# version 79
# baseline (speedup 1.0000x reference)
import contextlib
from collections import defaultdict

import numpy as np
import ml_dtypes

import concourse.bass as bass
import concourse.mybir as mybir
from concourse.bass_utils import run_bass_kernel_spmd
from concourse.alu_op_type import AluOpType as ALU

F32 = mybir.dt.float32
BF16 = mybir.dt.bfloat16
F8 = mybir.dt.float8e4
AF = mybir.ActivationFunctionType
AX = mybir.AxisListType

N_CORES = 8
H = 1024
HC = H // 128
EPS = 1e-5

ENGINES = ("sync", "vector", "scalar", "tensor", "gpsimd")
ENG_SEM = {"vector": "sv", "scalar": "sa", "tensor": "sp", "gpsimd": "sgc"}


class Plan:

    def __init__(self):
        self.streams = {e: [] for e in ENGINES}
        self.semcnt = defaultdict(int)
        self.hw = {e: defaultdict(int) for e in ENGINES}
        self.bufw = {}
        self.bufr = defaultdict(list)

    def wait(self, eng, sem, cnt):
        if cnt <= self.hw[eng][sem]:
            return
        self.hw[eng][sem] = cnt
        self.streams[eng].append(("w", sem, cnt))

    def op(self, eng, fn, reads=(), writes=(), sem=None, inc=1, self_inc=True):
        need = {}
        def _need(sem2, cnt2):
            if cnt2 > need.get(sem2, -1):
                need[sem2] = cnt2
        for b in reads:
            lw = self.bufw.get(b)
            if lw:
                _need(*lw)
        for b in writes:
            lw = self.bufw.get(b)
            if lw:
                _need(*lw)
            for r in self.bufr.get(b, ()):
                _need(*r)
        for sem2, cnt2 in need.items():
            self.wait(eng, sem2, cnt2)
        s = sem if sem is not None else ENG_SEM[eng]
        self.semcnt[s] += inc
        c = self.semcnt[s]
        self.streams[eng].append(("o", fn, s, inc, self_inc))
        for b in reads:
            self.bufr[b].append((s, c))
        for b in writes:
            self.bufw[b] = (s, c)
            self.bufr[b] = []
        return (s, c)


def build_nc(B, S):
    T = S // 128
    NQ = T // 4
    SC_CH = S // 512

    nc = bass.Bass("TRN2", target_bir_lowering=False)

    x_h = nc.dram_tensor("x", [B, S, H], F32, kind="ExternalInput").ap()
    wt_h = nc.dram_tensor("wt", [H, H], BF16, kind="ExternalInput").ap()
    b_h = nc.dram_tensor("b", [H], F32, kind="ExternalInput").ap()
    ker_h = nc.dram_tensor("ker", [H, H], BF16, kind="ExternalInput").ap()
    sc_h = nc.dram_tensor("sc", [2], F32, kind="ExternalInput").ap()
    fb_h = nc.dram_tensor("fb", [H], BF16, kind="ExternalInput").ap()
    id_h = nc.dram_tensor("ids", [128, 128], BF16, kind="ExternalInput").ap()
    y_h = nc.dram_tensor("y", [B, S, H], F32, kind="ExternalOutput").ap()

    x_t = x_h.rearrange("s (t p) h -> s t p h", p=128)
    y_t = y_h.rearrange("s (t p) h -> s t p h", p=128)
    wt_c = wt_h.rearrange("(c p) o -> p c o", p=128)
    ker_c = ker_h.rearrange("(c p) o -> p c o", p=128)
    b_c = b_h.rearrange("(c p one) -> c p one", c=HC, one=1)
    sc_r = sc_h.rearrange("(one s) -> one s", one=1)
    fb_r = fb_h.rearrange("(one h) -> one h", one=1)

    P = Plan()

    with contextlib.ExitStack() as es:
        e = es.enter_context
        WTs = e(nc.sbuf_tensor("WTs", [128, HC, H], BF16))
        kst = e(nc.sbuf_tensor("kst", [128, 2, 2, H], BF16))
        kq = e(nc.sbuf_tensor("kq", [128, HC, H], F8))
        Vb = e(nc.sbuf_tensor("Vb", [128, T, H], BF16))
        qT = e(nc.sbuf_tensor("qT", [128, HC, S], BF16))
        lnT = e(nc.sbuf_tensor("lnT", [128, 2, HC, 512], BF16))
        xa = e(nc.sbuf_tensor("xa", [128, 4, H], F32))
        attn = e(nc.sbuf_tensor("attn", [128, 2, S], BF16))
        attnT = e(nc.sbuf_tensor("attnT", [128, 3, T, 128], BF16))
        x2 = e(nc.sbuf_tensor("x2", [128, 4, H], F32))
        ln2 = e(nc.sbuf_tensor("ln2", [128, 2, H], BF16))
        ln2T = e(nc.sbuf_tensor("ln2T", [128, 2, HC, 128], BF16))
        ln2q = e(nc.sbuf_tensor("ln2q", [128, 4, HC, 128], F8))
        y1 = e(nc.sbuf_tensor("y1", [128, 2, H], BF16))
        yy = e(nc.sbuf_tensor("yy", [128, 2, H], BF16))
        xo = e(nc.sbuf_tensor("xo", [128, 2, H], F32))
        fbb = e(nc.sbuf_tensor("fbb", [128, H], BF16))
        bvec = e(nc.sbuf_tensor("bvec", [128, HC], F32))
        scs = e(nc.sbuf_tensor("scs", [128, 6], F32))
        st1 = e(nc.sbuf_tensor("st1", [128, 4, 2, 6], F32))
        xb = e(nc.sbuf_tensor("xb", [128, 2, H], F32))
        st1b = e(nc.sbuf_tensor("st1b", [128, 2, 2, 6], F32))
        mv1b = e(nc.sbuf_tensor("mv1b", [128, T, 2], F32))
        rsd1b = e(nc.sbuf_tensor("rsd1b", [128, T, 2], F32))
        mv1 = e(nc.sbuf_tensor("mv1", [128, 4, 2], F32))
        rsd1 = e(nc.sbuf_tensor("rsd1", [128, 4, 2], F32))
        st2 = e(nc.sbuf_tensor("st2", [128, 2, 2, 6], F32))
        mv2 = e(nc.sbuf_tensor("mv2", [128, 2, 2], F32))
        rsd2 = e(nc.sbuf_tensor("rsd2", [128, 2, 2], F32))
        ngm = e(nc.sbuf_tensor("ngm", [128, 2, 1], F32))
        ngm4 = e(nc.sbuf_tensor("ngm4", [128, 2, 4], F32))
        zz = e(nc.sbuf_tensor("zz", [128, 3, 2], F32))
        rz = e(nc.sbuf_tensor("rz", [128, 2, 1], F32))
        epsc = e(nc.sbuf_tensor("epsc", [128, 1], F32))
        ones32 = e(nc.sbuf_tensor("ones32", [1, 128], F32))
        onesbf = e(nc.sbuf_tensor("onesbf", [1, 512], BF16))
        scrow = e(nc.sbuf_tensor("scrow", [1, 2], F32))
        fbrow = e(nc.sbuf_tensor("fbrow", [1, H], BF16))
        ids = e(nc.sbuf_tensor("ids_s", [128, 128], BF16))

        p_sc = e(nc.psum_tensor("p_sc", [128, 4 * 512], F32))
        p_cx = e(nc.psum_tensor("p_cx", [128, 2 * 512], F32))
        p_ff = e(nc.psum_tensor("p_ff", [128, 2 * 512], F32))
        BANK = ([p_sc[:, i * 512:(i + 1) * 512] for i in range(4)]
                + [p_cx[:, i * 512:(i + 1) * 512] for i in range(2)]
                + [p_ff[:, i * 512:(i + 1) * 512] for i in range(2)])

        sem_names = ["sv", "sa", "sp", "sgc",
                     "sxa0", "sxa1", "sxa2", "sxa3", "sxb0", "sxb1", "sst0", "sst1",
                     "strV0", "strV1", "strA0", "strA1", "strA2", "strL0", "strL1",
                     "swk0", "swk1", "swk2", "swk3", "sww", "swb",
                     "sy0", "sy1", "sy2", "sy3", "ssy"]
        semh = {n: e(nc.semaphore(n)) for n in sem_names}

        def dma(queue, out_ap, in_ap, sem, reads=(), writes=()):
            def fn(g, out_ap=out_ap, in_ap=in_ap):
                return g.dma_start(out=out_ap, in_=in_ap)
            P.op(queue, fn, reads=reads, writes=writes, sem=sem, inc=16)

        def dma_tr(out_ap, in_ap, sem, reads=(), writes=()):
            def fn(g, out_ap=out_ap, in_ap=in_ap):
                return g.dma_start_transpose(out=out_ap, in_=in_ap)
            P.op("sync", fn, reads=reads, writes=writes, sem=sem, inc=16)

        def init_loads(g):
            for o in range(HC):
                g.dma_start(out=bvec[:, o:o + 1], in_=b_c[o]).then_inc(semh["swb"], 16)
            g.dma_start(out=scrow[:], in_=sc_r).then_inc(semh["swb"], 16)
            g.dma_start(out=fbrow[:], in_=fb_r).then_inc(semh["swb"], 16)
            g.dma_start(out=ids[:], in_=id_h[:]).then_inc(semh["swb"], 16)
            return None
        P.op("sync", init_loads, writes=("bvec", "scrow", "fbrow", "ids"),
             sem="swb", inc=16 * (HC + 3), self_inc=False)

        P.op("vector", lambda v: v.memset(epsc[:], EPS), writes=("epsc",))
        P.op("scalar", lambda sce: sce.activation(out=scs[:, 3:4], in_=epsc[:],
                                                  func=AF.Ln, bias=epsc[:]),
             reads=("epsc",), writes=("warm",))
        P.op("vector", lambda v: v.memset(ones32[:], 1.0), writes=("ones32",))
        P.op("vector", lambda v: v.memset(onesbf[:], 1.0), writes=("onesbf",))
        P.op("tensor", lambda t: t.matmul(p_cx[:, 0:2], ones32[:], scrow[:],
                                          start=True, stop=True),
             reads=("ones32", "scrow"), writes=("pb4",))
        P.op("vector", lambda v: v.tensor_copy(out=scs[:, 4:6], in_=p_cx[:, 0:2]),
             reads=("pb4",), writes=("scs",))
        P.op("vector", lambda v: v.reciprocal(out=scs[:, 0:1], in_=scs[:, 4:5]),
             reads=("scs",), writes=("scs",))
        P.op("vector", lambda v: v.reciprocal(out=scs[:, 1:2], in_=scs[:, 5:6]),
             reads=("scs",), writes=("scs",))
        P.op("vector", lambda v: v.tensor_tensor(out=scs[:, 2:3], in0=scs[:, 4:5],
                                                 in1=scs[:, 5:6], op=ALU.mult),
             reads=("scs",), writes=("scs",))
        def fb_mm(t):
            t.matmul(p_ff[:, 0:512], onesbf[:, 0:128], fbrow[:, 0:512], start=True, stop=True)
            return t.matmul(p_ff[:, 512:1024], onesbf[:, 0:128], fbrow[:, 512:1024],
                            start=True, stop=True)
        P.op("tensor", fb_mm, reads=("onesbf", "fbrow"), writes=("pb6", "pb7"))

        def plan_fbb_evac():
            P.op("scalar", lambda sce: sce.activation(out=fbb[:], in_=p_ff[:, 0:1024],
                                                      func=AF.Copy),
                 reads=("pb6", "pb7"), writes=("fbb",))

        def plan_kq(j):
            sl = j % 2
            dma("scalar", kst[:, sl], ker_c[:, 2 * j:2 * j + 2, :], f"swk{sl}",
                writes=(f"kst{sl}",))
        def plan_kquant(j, h):
            def qfn(sce, j=j, h=h):
                return sce.activation(out=kq[:, 2 * j + h, :], in_=kst[:, j % 2, h],
                                      func=AF.Copy, scale=scs[:, 1:2])
            P.op("scalar", qfn, reads=(f"kst{j % 2}", "scs"), writes=("kq",))

        def ln_stats(slot, src_ap, stt, mvt, rsdt, keys):
            xkey, stkey, mvkey, rsdkey = keys
            P.op("vector", lambda v: v.bn_stats(out=stt[:, slot, 0], in_=src_ap[:, 0:512]),
                 reads=(xkey,), writes=(stkey,))
            P.op("vector", lambda v: v.bn_stats(out=stt[:, slot, 1], in_=src_ap[:, 512:1024]),
                 reads=(xkey,), writes=(stkey,))
            P.op("vector", lambda v: v.bn_aggr(out=mvt[:, slot], in_=stt[:, slot]),
                 reads=(stkey,), writes=(mvkey,))
            P.op("scalar", lambda sce: sce.activation(out=rsdt[:, slot, 0:1],
                                                      in_=mvt[:, slot, 1:2],
                                                      func=AF.Ln, bias=epsc[:]),
                 reads=(mvkey, "epsc"), writes=(rsdkey,))
            P.op("scalar", lambda sce: sce.activation(out=rsdt[:, slot, 1:2],
                                                      in_=rsdt[:, slot, 0:1],
                                                      func=AF.Exp, scale=-0.5),
                 reads=(rsdkey,), writes=(rsdkey,))

        NLD = 4

        def plan_A_load(s, t):
            slot = t % 4
            dma("gpsimd", xa[:, slot], x_t[s, t], f"sxa{slot}", writes=(f"xa{slot}",))

        for s in range(B):
            have_stats = s >= 1
            VA = 1 if have_stats else 2
            for t in range(T + VA + 6):
                if t < T and not (have_stats and t < NLD):
                    slot = t % 4
                    if slot % 2 == 0 and not (have_stats and t in (4, 6)):
                        dma("gpsimd", xa[:, slot], x_t[s, t], f"sxa{slot}",
                            writes=(f"xa{slot}",))
                    else:
                        dma("sync", xa[:, slot], x_t[s, t], f"sy{slot}",
                            writes=(f"xa{slot}",))
                if s == 0 and t < 4:
                    dma("scalar", WTs[:, 2 * t:2 * t + 2, :], wt_c[:, 2 * t:2 * t + 2, :],
                        "sww", writes=("WTs",))
                if VA <= t < T + VA:
                    ta = t - VA
                    slot = ta % 4
                    if have_stats:
                        P.op("vector", lambda v, ta=ta, slot=slot: v.tensor_scalar(
                                out=Vb[:, ta, :], in0=xa[:, slot],
                                scalar1=mv1b[:, ta, 0:1], scalar2=rsd1b[:, ta, 1:2],
                                op0=ALU.subtract, op1=ALU.mult),
                             reads=(f"xa{slot}", f"rsd1b{ta}", f"mv1b{ta}"),
                             writes=(f"Vb{ta}",))
                    else:
                        P.op("vector", lambda v, ta=ta, slot=slot: v.tensor_scalar(
                                out=Vb[:, ta, :], in0=xa[:, slot],
                                scalar1=mv1[:, slot, 0:1], scalar2=rsd1[:, slot, 1:2],
                                op0=ALU.subtract, op1=ALU.mult),
                             reads=(f"xa{slot}", f"rsd1{slot}", f"mv1{slot}"), writes=(f"Vb{ta}",))
                if t >= VA + 1:
                    tv = t - VA - 1
                    if tv < T + 4:
                        q = tv // 4
                        r = tv % 4
                        if tv < T:
                            pp = p_cx if tv % 2 == 0 else p_ff
                            pk = ("pb4", "pb5") if tv % 2 == 0 else ("pb6", "pb7")
                            def vtr(te, tv=tv, pp=pp):
                                last = None
                                for hh in range(HC):
                                    last = te.matmul(pp[:, hh * 128:(hh + 1) * 128],
                                                     Vb[:, tv, hh * 128:(hh + 1) * 128],
                                                     ids[:], start=True, stop=True)
                                return last
                            P.op("tensor", vtr, reads=(f"Vb{tv}", "ids"), writes=pk)
                            P.op("scalar", lambda sce, q=q, r=r, pp=pp: sce.activation(
                                    out=lnT[:, q % 2, :, r * 128:(r + 1) * 128],
                                    in_=pp[:, 0:1024], func=AF.Copy),
                                 reads=pk, writes=(f"lnT{q % 2}_{r}",))
                        qprev = tv // 4 - 1 + (1 if tv % 4 == 3 else 0)
                        if tv % 4 == 3:
                            qprev = None
                        qp_q = (tv - 4) // 4
                        if qp_q >= 0:
                            rr = tv % 4
                            if qp_q == 0 or qp_q == T // 4 - 1:
                                olist = (0, 1, 2, 3) if rr == 0 else (
                                    (4, 5, 6, 7) if rr == 1 else ())
                            else:
                                olist = (2 * rr, 2 * rr + 1)
                            for o in olist:
                                def qp(te, o=o, q=qp_q):
                                    last = None
                                    for hh in range(HC):
                                        last = te.matmul(BANK[o % 4],
                                                         WTs[:, hh, o * 128:(o + 1) * 128],
                                                         lnT[:, q % 2, hh, :],
                                                         start=(hh == 0), stop=(hh == HC - 1))
                                    return last
                                P.op("tensor", qp,
                                     reads=tuple(f"lnT{qp_q % 2}_{rr2}" for rr2 in range(4))
                                           + ("WTs",),
                                     writes=(f"pb{o % 4}",))
                                P.op("scalar", lambda sce, o=o, q=qp_q: sce.activation(
                                        out=qT[:, o, q * 512:(q + 1) * 512], in_=BANK[o % 4],
                                        func=AF.Identity, bias=bvec[:, o:o + 1]),
                                     reads=(f"pb{o % 4}", "bvec"), writes=(f"qT{qp_q}",))
                if t < T and not have_stats:
                    slot = t % 4
                    ln_stats(slot, xa[:, slot], st1, mv1, rsd1,
                             (f"xa{slot}", f"st1{slot}", f"mv1{slot}", f"rsd1{slot}"))
                if s == 0:
                    if t == 1:
                        plan_fbb_evac()
                    if t in (1, 3, 7, 11):
                        plan_kq({1: 0, 3: 1, 7: 2, 11: 3}[t])
                    KQS = {5: [(0, 0)], 6: [(0, 1)], 9: [(1, 0)], 10: [(1, 1)],
                           13: [(2, 0)], 14: [(2, 1)]}
                    t3a = 16 if T + VA + 6 > 17 else T + VA + 4
                    t3b = 17 if T + VA + 6 > 17 else T + VA + 5
                    KQS.setdefault(t3a, []).append((3, 0))
                    KQS.setdefault(t3b, []).append((3, 1))
                    for jh in KQS.get(t, ()):
                        plan_kquant(*jh)

            l2tr_pending = None
            pref_rsd_pending = None
            for it in range(T + 4):
                l2tr_flush, l2tr_pending = l2tr_pending, None
                pref_flush, pref_rsd_pending = pref_rsd_pending, None
                if s + 1 < B and T - 1 <= it < T - 1 + NLD:
                    plan_A_load(s + 1, it - (T - 1))

                def emit_stage1(it=it):
                    if it >= T:
                        return
                    t0 = it
                    sl0 = t0 % 2
                    sa0 = t0 % 3
                    sx0 = t0 % 4
                    bko = 4 if t0 == 1 else 0
                    for c in range(SC_CH):
                        def sc_mm(te, t0=t0, c=c, bko=bko):
                            last = None
                            for hh in range(HC):
                                last = te.matmul(BANK[bko + c],
                                                 qT[:, hh, t0 * 128:(t0 + 1) * 128],
                                                 qT[:, hh, c * 512:(c + 1) * 512],
                                                 start=(hh == 0), stop=(hh == HC - 1))
                            return last
                        P.op("tensor", sc_mm,
                             reads=(f"qT{c}", f"qT{t0 // 4}"), writes=(f"pb{bko + c}",))
                        P.op("vector", lambda v, sl0=sl0, c=c, bko=bko: v.reduce_max(
                                out=ngm4[:, sl0, c:c + 1], in_=BANK[bko + c], axis=AX.X),
                             reads=(f"pb{bko + c}",), writes=(f"ngm4{sl0}",))
                    P.op("vector", lambda v, sl0=sl0: v.reduce_max(
                            out=ngm[:, sl0], in_=ngm4[:, sl0, 0:SC_CH], axis=AX.X,
                            negate=True),
                         reads=(f"ngm4{sl0}",), writes=(f"ngm{sl0}",))
                    if bko == 4 and SC_CH == 4:
                        P.op("scalar", lambda sce, sl0=sl0: sce.activation(
                                out=attn[:, sl0, 0:1024], in_=p_cx[:, 0:1024],
                                func=AF.Exp, bias=ngm[:, sl0],
                                accum_out=zz[:, sa0, 0:1]),
                             reads=("pb4", "pb5", f"ngm{sl0}"),
                             writes=(f"attn{sl0}", f"zz{sa0}"))
                        P.op("scalar", lambda sce, sl0=sl0: sce.activation(
                                out=attn[:, sl0, 1024:2048], in_=p_ff[:, 0:1024],
                                func=AF.Exp, bias=ngm[:, sl0],
                                accum_out=zz[:, sa0, 1:2]),
                             reads=("pb6", "pb7", f"ngm{sl0}"),
                             writes=(f"attn{sl0}", f"zzb{sa0}"))
                        P.op("vector", lambda v, sa0=sa0: v.tensor_tensor(
                                out=zz[:, sa0, 0:1], in0=zz[:, sa0, 0:1],
                                in1=zz[:, sa0, 1:2], op=ALU.add),
                             reads=(f"zz{sa0}", f"zzb{sa0}"), writes=(f"zz{sa0}",))
                    elif bko == 4:
                        P.op("scalar", lambda sce, sl0=sl0: sce.activation(
                                out=attn[:, sl0], in_=p_cx[:, 0:S], func=AF.Exp,
                                bias=ngm[:, sl0], accum_out=zz[:, sa0, 0:1]),
                             reads=tuple(f"pb{4 + c}" for c in range(SC_CH))
                                   + (f"ngm{sl0}",),
                             writes=(f"attn{sl0}", f"zz{sa0}"))
                    else:
                        P.op("scalar", lambda sce, sl0=sl0: sce.activation(
                                out=attn[:, sl0], in_=p_sc[:, 0:S], func=AF.Exp,
                                bias=ngm[:, sl0], accum_out=zz[:, sa0, 0:1]),
                             reads=tuple(f"pb{c}" for c in range(SC_CH)) + (f"ngm{sl0}",),
                             writes=(f"attn{sl0}", f"zz{sa0}"))
                    dma_tr(attnT[:, sa0], attn[:, sl0], f"strA{sa0}",
                           reads=(f"attn{sl0}",), writes=(f"attnT{sa0}",))
                    dma("gpsimd", xa[:, sx0], x_t[s, t0], f"sxa{sx0}", writes=(f"xa{sx0}",))

                def emit_stage2(it=it):
                    nonlocal l2tr_pending
                    if not (2 <= it <= T + 1):
                        return
                    t1 = it - 2
                    sl1 = t1 % 2
                    sa1 = t1 % 3
                    sx1 = t1 % 4
                    sq1 = t1 % 4
                    pcx, pck = p_cx[:, 0:1024], ("pb4", "pb5")
                    def cx_half(hf):
                        def fn(te, sa1=sa1, pcx=pcx, hf=hf):
                            last = None
                            for c in range(T):
                                last = te.matmul(pcx[:, hf * 512:hf * 512 + 512],
                                                 attnT[:, sa1, c, :],
                                                 Vb[:, c, hf * 512:hf * 512 + 512],
                                                 start=(c == 0), stop=(c == T - 1))
                            return last
                        return fn
                    P.op("vector", lambda v, sl1=sl1, sa1=sa1: v.reciprocal(
                            out=rz[:, sl1], in_=zz[:, sa1, 0:1]),
                         reads=(f"zz{sa1}",), writes=(f"rz{sl1}",))
                    if t1 >= T - 4:
                        P.op("tensor", cx_half(0),
                             reads=(f"attnT{sa1}",) + tuple(f"Vb{c}" for c in range(T)),
                             writes=(pck[0],))
                        P.op("tensor", cx_half(1),
                             reads=(f"attnT{sa1}",) + tuple(f"Vb{c}" for c in range(T)),
                             writes=(pck[1],))
                        for hf in range(2):
                            P.op("vector", lambda v, sl1=sl1, sx1=sx1, sq1=sq1,
                                 pcx=pcx, hf=hf: v.scalar_tensor_tensor(
                                    out=x2[:, sq1, hf * 512:hf * 512 + 512],
                                    in0=pcx[:, hf * 512:hf * 512 + 512],
                                    scalar=rz[:, sl1],
                                    in1=xa[:, sx1, hf * 512:hf * 512 + 512],
                                    op0=ALU.mult, op1=ALU.add),
                                 reads=(pck[hf], f"rz{sl1}", f"xa{sx1}"),
                                 writes=(f"x2{sq1}h{hf}", f"x2{sq1}"))
                            P.op("vector", lambda v, sl1=sl1, sq1=sq1, hf=hf: v.bn_stats(
                                    out=st2[:, sl1, hf],
                                    in_=x2[:, sq1, hf * 512:hf * 512 + 512]),
                                 reads=(f"x2{sq1}h{hf}",), writes=(f"st2{sl1}",))
                        P.op("vector", lambda v, sl1=sl1: v.bn_aggr(
                                out=mv2[:, sl1], in_=st2[:, sl1]),
                             reads=(f"st2{sl1}",), writes=(f"mv2{sl1}",))
                        P.op("scalar", lambda sce, sl1=sl1: sce.activation(
                                out=rsd2[:, sl1, 0:1], in_=mv2[:, sl1, 1:2],
                                func=AF.Ln, bias=epsc[:]),
                             reads=(f"mv2{sl1}", "epsc"), writes=(f"rsd2{sl1}",))
                        P.op("scalar", lambda sce, sl1=sl1: sce.activation(
                                out=rsd2[:, sl1, 1:2], in_=rsd2[:, sl1, 0:1],
                                func=AF.Exp, scale=-0.5),
                             reads=(f"rsd2{sl1}",), writes=(f"rsd2{sl1}",))
                    else:
                        P.op("tensor", lambda te, sa1=sa1, pcx=pcx: (
                                cx_half(0)(te, sa1=sa1, pcx=pcx),
                                cx_half(1)(te, sa1=sa1, pcx=pcx))[-1],
                             reads=(f"attnT{sa1}",) + tuple(f"Vb{c}" for c in range(T)),
                             writes=pck)
                        P.op("vector", lambda v, sl1=sl1, sx1=sx1, sq1=sq1, pcx=pcx: v.scalar_tensor_tensor(
                                out=x2[:, sq1], in0=pcx[:, 0:1024], scalar=rz[:, sl1],
                                in1=xa[:, sx1], op0=ALU.mult, op1=ALU.add),
                             reads=pck + (f"rz{sl1}", f"xa{sx1}"),
                             writes=(f"x2{sq1}",))
                        ln_stats(sl1, x2[:, sq1], st2, mv2, rsd2,
                                 (f"x2{sq1}", f"st2{sl1}", f"mv2{sl1}", f"rsd2{sl1}"))
                    P.op("vector", lambda v, sl1=sl1, sq1=sq1: v.tensor_scalar(
                            out=ln2[:, sl1], in0=x2[:, sq1],
                            scalar1=mv2[:, sl1, 0:1], scalar2=rsd2[:, sl1, 1:2],
                            op0=ALU.subtract, op1=ALU.mult),
                         reads=(f"x2{sq1}", f"rsd2{sl1}", f"mv2{sl1}"),
                         writes=(f"ln2{sl1}",))
                    if t1 >= T - 4:
                        pbq = p_sc[:, 0:1024] if t1 % 2 == 0 else p_sc[:, 1024:2048]
                        pkq = ("pb0", "pb1") if t1 % 2 == 0 else ("pb2", "pb3")
                        def emit_l2tr(sl1=sl1, sq1=sq1, pbq=pbq, pkq=pkq,
                                      split=(t1 >= T - 4)):
                            def l2tr(te, sl1=sl1, pbq=pbq):
                                last = None
                                for hh in range(HC):
                                    last = te.matmul(pbq[:, hh * 128:(hh + 1) * 128],
                                                     ln2[:, sl1, hh * 128:(hh + 1) * 128],
                                                     ids[:], start=True, stop=True)
                                return last
                            P.op("tensor", l2tr, reads=(f"ln2{sl1}", "ids"), writes=pkq)
                            if split:
                                for hf in range(2):
                                    P.op("scalar", lambda sce, sq1=sq1, pbq=pbq,
                                         hf=hf: sce.activation(
                                            out=ln2q[:, sq1, 4 * hf:4 * hf + 4, :],
                                            in_=pbq[:, hf * 512:hf * 512 + 512],
                                            func=AF.Copy, scale=scs[:, 0:1]),
                                         reads=(pkq[hf], "scs"),
                                         writes=(f"ln2q{sq1}h{hf}", f"ln2q{sq1}"))
                            else:
                                P.op("scalar", lambda sce, sq1=sq1, pbq=pbq: sce.activation(
                                        out=ln2q[:, sq1], in_=pbq[:, 0:1024], func=AF.Copy,
                                        scale=scs[:, 0:1]),
                                     reads=pkq + ("scs",),
                                     writes=(f"ln2q{sq1}", f"ln2q{sq1}h0",
                                             f"ln2q{sq1}h1"))
                        l2tr_pending = emit_l2tr
                    else:
                        dma_tr(ln2T[:, sl1], ln2[:, sl1], f"strL{sl1}",
                               reads=(f"ln2{sl1}",), writes=(f"ln2T{sl1}",))
                        P.op("scalar", lambda sce, sl1=sl1, sq1=sq1: sce.activation(
                                out=ln2q[:, sq1], in_=ln2T[:, sl1], func=AF.Copy,
                                scale=scs[:, 0:1]),
                             reads=(f"ln2T{sl1}", "scs"),
                             writes=(f"ln2q{sq1}", f"ln2q{sq1}h0", f"ln2q{sq1}h1"))

                emit_stage1()
                emit_stage2()

                if l2tr_flush is not None and it <= T + 1:
                    l2tr_flush()
                    l2tr_flush = None

                if s + 1 < B:
                    kl = it
                    if kl < T:
                        dma("gpsimd", xb[:, kl % 2], x_t[s + 1, kl], f"sxb{kl % 2}",
                            writes=(f"xb{kl % 2}",))
                    ks = it - 1
                    if 0 <= ks < T:
                        kslot = ks % 2
                        P.op("vector", lambda v, ks=ks, kslot=kslot: v.bn_stats(
                                out=st1b[:, kslot, 0], in_=xb[:, kslot, 0:512]),
                             reads=(f"xb{kslot}",), writes=(f"st1b{kslot}",))
                        P.op("vector", lambda v, ks=ks, kslot=kslot: v.bn_stats(
                                out=st1b[:, kslot, 1], in_=xb[:, kslot, 512:1024]),
                             reads=(f"xb{kslot}",), writes=(f"st1b{kslot}",))
                        P.op("vector", lambda v, ks=ks, kslot=kslot: v.bn_aggr(
                                out=mv1b[:, ks], in_=st1b[:, kslot]),
                             reads=(f"st1b{kslot}",), writes=(f"mv1b{ks}",))
                        def emit_pref_rsd(ks=ks):
                            P.op("scalar", lambda sce, ks=ks: sce.activation(
                                    out=rsd1b[:, ks, 0:1], in_=mv1b[:, ks, 1:2],
                                    func=AF.Ln, bias=epsc[:]),
                                 reads=(f"mv1b{ks}", "epsc"), writes=(f"rsd1b{ks}",))
                            P.op("scalar", lambda sce, ks=ks: sce.activation(
                                    out=rsd1b[:, ks, 1:2], in_=rsd1b[:, ks, 0:1],
                                    func=AF.Exp, scale=-0.5),
                                 reads=(f"rsd1b{ks}",), writes=(f"rsd1b{ks}",))
                        pref_rsd_pending = emit_pref_rsd

                if 4 <= it <= T + 3:
                    t2 = it - 4
                    sl2 = t2 % 2
                    sq2 = t2 % 4
                    DR = mybir.MatmulPerfMode.DoubleRow
                    def ff_mm(te, sq2=sq2):
                        last = None
                        for j in range(HC // 2):
                            last = te.matmul(p_ff[:, 0:512], ln2q[:, sq2, 2 * j:2 * j + 2, :],
                                             kq[:, 2 * j:2 * j + 2, 0:512],
                                             start=(j == 0), stop=(j == HC // 2 - 1),
                                             perf_mode=DR)
                        for j in range(HC // 2):
                            last = te.matmul(p_ff[:, 512:1024], ln2q[:, sq2, 2 * j:2 * j + 2, :],
                                             kq[:, 2 * j:2 * j + 2, 512:1024],
                                             start=(j == 0), stop=(j == HC // 2 - 1),
                                             perf_mode=DR)
                        return last
                    if t2 >= T - 4:
                        for hf in range(2):
                            def ff_h(te, sq2=sq2, hf=hf):
                                last = None
                                for j in (2 * hf, 2 * hf + 1):
                                    last = te.matmul(p_ff[:, 0:512],
                                                     ln2q[:, sq2, 2 * j:2 * j + 2, :],
                                                     kq[:, 2 * j:2 * j + 2, 0:512],
                                                     start=(j == 0),
                                                     stop=(j == HC // 2 - 1),
                                                     perf_mode=DR)
                                for j in (2 * hf, 2 * hf + 1):
                                    last = te.matmul(p_ff[:, 512:1024],
                                                     ln2q[:, sq2, 2 * j:2 * j + 2, :],
                                                     kq[:, 2 * j:2 * j + 2, 512:1024],
                                                     start=(j == 0),
                                                     stop=(j == HC // 2 - 1),
                                                     perf_mode=DR)
                                return last
                            P.op("tensor", ff_h, reads=(f"ln2q{sq2}h{hf}", "kq"),
                                 writes=("pb6", "pb7"))
                    else:
                        P.op("tensor", ff_mm, reads=(f"ln2q{sq2}", "kq"),
                             writes=("pb6", "pb7"))
                    P.op("scalar", lambda sce, sl2=sl2: sce.activation(
                            out=y1[:, sl2], in_=p_ff[:, 0:1024], func=AF.Copy,
                            scale=scs[:, 2:3]),
                         reads=("pb6", "pb7", "scs"), writes=(f"y1{sl2}",))
                    if s == B - 1 and t2 >= T - 2:
                        P.op("vector", lambda v, sl2=sl2: v.tensor_tensor(
                                out=yy[:, sl2], in0=y1[:, sl2], in1=fbb[:],
                                op=ALU.add),
                             reads=(f"y1{sl2}", "fbb"), writes=(f"yy{sl2}",))
                        for h in range(2):
                            P.op("vector", lambda v, sl2=sl2, sq2=sq2, h=h:
                                 v.tensor_tensor(
                                    out=xo[:, sl2, h * 512:(h + 1) * 512],
                                    in0=x2[:, sq2, h * 512:(h + 1) * 512],
                                    in1=yy[:, sl2, h * 512:(h + 1) * 512],
                                    op=ALU.add),
                                 reads=(f"x2{sq2}", f"yy{sl2}"),
                                 writes=(f"xo{sl2}h{h}", f"xo{sl2}"))
                        dma("gpsimd", y_t[s, t2][:, 0:512], xo[:, sl2, 0:512],
                            f"sst{sl2}", reads=(f"xo{sl2}h0",))
                        dma("sync", y_t[s, t2][:, 512:1024], xo[:, sl2, 512:1024],
                            "ssy", reads=(f"xo{sl2}h1",))
                    else:
                        P.op("gpsimd", lambda g, sl2=sl2: g.tensor_tensor(
                                out=yy[:, sl2], in0=y1[:, sl2], in1=fbb[:], op=ALU.add),
                             reads=(f"y1{sl2}", "fbb"), writes=(f"yy{sl2}",))
                        P.op("gpsimd", lambda g, sl2=sl2, sq2=sq2: g.tensor_tensor(
                                out=xo[:, sl2], in0=x2[:, sq2], in1=yy[:, sl2], op=ALU.add),
                             reads=(f"x2{sq2}", f"yy{sl2}"), writes=(f"xo{sl2}",))
                        dma("gpsimd", y_t[s, t2], xo[:, sl2], f"sst{sl2}",
                            reads=(f"xo{sl2}",))

                if pref_flush is not None:
                    pref_flush()

                if l2tr_flush is not None:
                    l2tr_flush()

        P.wait("gpsimd", "sst0", P.semcnt["sst0"])
        P.wait("gpsimd", "sst1", P.semcnt["sst1"])
        P.wait("gpsimd", "ssy", P.semcnt["ssy"])

        with nc.Block() as block:
            def mk(eng_name):
                def run(eng):
                    for item in P.streams[eng_name]:
                        if item[0] == "w":
                            eng.wait_ge(semh[item[1]], item[2])
                        else:
                            _, fn, semn, inc, self_inc = item
                            instr = fn(eng)
                            if self_inc:
                                instr.then_inc(semh[semn], inc)
                return run
            block.sync(mk("sync"))
            block.vector(mk("vector"))
            block.scalar(mk("scalar"))
            block.tensor(mk("tensor"))
            block.gpsimd(mk("gpsimd"))

    return nc


def kernel(L0_attn_lstm_out, L0_attn_W, L0_attn_b, L0_ffn_kernel,
           L0_ffn_input_scale, L0_ffn_kernel_scale, L0_ffn_bias):
    bf16 = ml_dtypes.bfloat16
    x = np.ascontiguousarray(L0_attn_lstm_out, dtype=np.float32)
    WT = np.ascontiguousarray(np.asarray(L0_attn_W, dtype=np.float32).T.astype(bf16))
    b = np.ascontiguousarray(L0_attn_b, dtype=np.float32)
    ker = np.ascontiguousarray(np.asarray(L0_ffn_kernel).astype(bf16))
    sc = np.array([np.float32(L0_ffn_input_scale[0]),
                   np.float32(L0_ffn_kernel_scale[0])], dtype=np.float32)
    fb = np.ascontiguousarray(np.asarray(L0_ffn_bias).astype(bf16))
    ids = np.eye(128, dtype=bf16)

    BATCH, S, _ = x.shape
    per = BATCH // N_CORES
    nc = build_nc(per, S)
    in_maps = []
    for i in range(N_CORES):
        in_maps.append({
            "x": np.ascontiguousarray(x[i * per:(i + 1) * per]),
            "wt": WT, "b": b, "ker": ker, "sc": sc, "fb": fb, "ids": ids,
        })
    res = run_bass_kernel_spmd(nc, in_maps, core_ids=list(range(N_CORES)))
    out = np.concatenate([np.asarray(r["y"]) for r in res.results], axis=0)
    return out.astype(np.float32)


# version 80
# speedup vs baseline: 1.0232x; 1.0232x over previous
import contextlib
from collections import defaultdict

import numpy as np
import ml_dtypes

import concourse.bass as bass
import concourse.mybir as mybir
from concourse.bass_utils import run_bass_kernel_spmd
from concourse.alu_op_type import AluOpType as ALU

F32 = mybir.dt.float32
BF16 = mybir.dt.bfloat16
F8 = mybir.dt.float8e4
AF = mybir.ActivationFunctionType
AX = mybir.AxisListType

N_CORES = 8
H = 1024
HC = H // 128
EPS = 1e-5

ENGINES = ("sync", "vector", "scalar", "tensor", "gpsimd")
ENG_SEM = {"vector": "sv", "scalar": "sa", "tensor": "sp", "gpsimd": "sgc"}


class Plan:

    def __init__(self):
        self.streams = {e: [] for e in ENGINES}
        self.semcnt = defaultdict(int)
        self.hw = {e: defaultdict(int) for e in ENGINES}
        self.bufw = {}
        self.bufr = defaultdict(list)

    def wait(self, eng, sem, cnt):
        if cnt <= self.hw[eng][sem]:
            return
        self.hw[eng][sem] = cnt
        self.streams[eng].append(("w", sem, cnt))

    def op(self, eng, fn, reads=(), writes=(), sem=None, inc=1, self_inc=True):
        need = {}
        def _need(sem2, cnt2):
            if cnt2 > need.get(sem2, -1):
                need[sem2] = cnt2
        for b in reads:
            lw = self.bufw.get(b)
            if lw:
                _need(*lw)
        for b in writes:
            lw = self.bufw.get(b)
            if lw:
                _need(*lw)
            for r in self.bufr.get(b, ()):
                _need(*r)
        for sem2, cnt2 in need.items():
            self.wait(eng, sem2, cnt2)
        s = sem if sem is not None else ENG_SEM[eng]
        self.semcnt[s] += inc
        c = self.semcnt[s]
        self.streams[eng].append(("o", fn, s, inc, self_inc))
        for b in reads:
            self.bufr[b].append((s, c))
        for b in writes:
            self.bufw[b] = (s, c)
            self.bufr[b] = []
        return (s, c)


def build_nc(B, S):
    T = S // 128
    NQ = T // 4
    SC_CH = S // 512

    nc = bass.Bass("TRN2", target_bir_lowering=False)

    x_h = nc.dram_tensor("x", [B, S, H], F32, kind="ExternalInput").ap()
    wt_h = nc.dram_tensor("wt", [H, H], BF16, kind="ExternalInput").ap()
    b_h = nc.dram_tensor("b", [H], F32, kind="ExternalInput").ap()
    ker_h = nc.dram_tensor("ker", [H, H], BF16, kind="ExternalInput").ap()
    sc_h = nc.dram_tensor("sc", [2], F32, kind="ExternalInput").ap()
    fb_h = nc.dram_tensor("fb", [H], BF16, kind="ExternalInput").ap()
    id_h = nc.dram_tensor("ids", [128, 128], BF16, kind="ExternalInput").ap()
    y_h = nc.dram_tensor("y", [B, S, H], F32, kind="ExternalOutput").ap()

    x_t = x_h.rearrange("s (t p) h -> s t p h", p=128)
    y_t = y_h.rearrange("s (t p) h -> s t p h", p=128)
    wt_c = wt_h.rearrange("(c p) o -> p c o", p=128)
    ker_c = ker_h.rearrange("(c p) o -> p c o", p=128)
    b_c = b_h.rearrange("(c p one) -> c p one", c=HC, one=1)
    sc_r = sc_h.rearrange("(one s) -> one s", one=1)
    fb_r = fb_h.rearrange("(one h) -> one h", one=1)

    P = Plan()

    with contextlib.ExitStack() as es:
        e = es.enter_context
        WTs = e(nc.sbuf_tensor("WTs", [128, HC, H], BF16))
        kst = e(nc.sbuf_tensor("kst", [128, 2, 2, H], BF16))
        kq = e(nc.sbuf_tensor("kq", [128, HC, H], F8))
        Vb = e(nc.sbuf_tensor("Vb", [128, T, H], BF16))
        qT = e(nc.sbuf_tensor("qT", [128, HC, S], BF16))
        lnT = e(nc.sbuf_tensor("lnT", [128, 2, HC, 512], BF16))
        xa = e(nc.sbuf_tensor("xa", [128, 4, H], F32))
        attn = e(nc.sbuf_tensor("attn", [128, 2, S], BF16))
        attnT = e(nc.sbuf_tensor("attnT", [128, 3, T, 128], BF16))
        x2 = e(nc.sbuf_tensor("x2", [128, 4, H], F32))
        ln2 = e(nc.sbuf_tensor("ln2", [128, 2, H], BF16))
        ln2T = e(nc.sbuf_tensor("ln2T", [128, 2, HC, 128], BF16))
        ln2q = e(nc.sbuf_tensor("ln2q", [128, 4, HC, 128], F8))
        y1 = e(nc.sbuf_tensor("y1", [128, 2, H], BF16))
        yy = e(nc.sbuf_tensor("yy", [128, 2, H], BF16))
        xo = e(nc.sbuf_tensor("xo", [128, 2, H], F32))
        fbb = e(nc.sbuf_tensor("fbb", [128, H], BF16))
        bvec = e(nc.sbuf_tensor("bvec", [128, HC], F32))
        scs = e(nc.sbuf_tensor("scs", [128, 6], F32))
        st1 = e(nc.sbuf_tensor("st1", [128, 4, 2, 6], F32))
        xb = e(nc.sbuf_tensor("xb", [128, 2, H], F32))
        st1b = e(nc.sbuf_tensor("st1b", [128, 2, 2, 6], F32))
        mv1b = e(nc.sbuf_tensor("mv1b", [128, T, 2], F32))
        rsd1b = e(nc.sbuf_tensor("rsd1b", [128, T, 2], F32))
        mv1 = e(nc.sbuf_tensor("mv1", [128, 4, 2], F32))
        rsd1 = e(nc.sbuf_tensor("rsd1", [128, 4, 2], F32))
        st2 = e(nc.sbuf_tensor("st2", [128, 2, 2, 6], F32))
        mv2 = e(nc.sbuf_tensor("mv2", [128, 2, 2], F32))
        rsd2 = e(nc.sbuf_tensor("rsd2", [128, 2, 2], F32))
        ngm = e(nc.sbuf_tensor("ngm", [128, 2, 1], F32))
        ngm4 = e(nc.sbuf_tensor("ngm4", [128, 2, 4], F32))
        zz = e(nc.sbuf_tensor("zz", [128, 3, 2], F32))
        rz = e(nc.sbuf_tensor("rz", [128, 2, 1], F32))
        epsc = e(nc.sbuf_tensor("epsc", [128, 1], F32))
        ones32 = e(nc.sbuf_tensor("ones32", [1, 128], F32))
        onesbf = e(nc.sbuf_tensor("onesbf", [1, 512], BF16))
        scrow = e(nc.sbuf_tensor("scrow", [1, 2], F32))
        fbrow = e(nc.sbuf_tensor("fbrow", [1, H], BF16))
        ids = e(nc.sbuf_tensor("ids_s", [128, 128], BF16))

        p_sc = e(nc.psum_tensor("p_sc", [128, 4 * 512], F32))
        p_cx = e(nc.psum_tensor("p_cx", [128, 2 * 512], F32))
        p_ff = e(nc.psum_tensor("p_ff", [128, 2 * 512], F32))
        BANK = ([p_sc[:, i * 512:(i + 1) * 512] for i in range(4)]
                + [p_cx[:, i * 512:(i + 1) * 512] for i in range(2)]
                + [p_ff[:, i * 512:(i + 1) * 512] for i in range(2)])

        sem_names = ["sv", "sa", "sp", "sgc",
                     "sxa0", "sxa1", "sxa2", "sxa3", "sxb0", "sxb1", "sst0", "sst1",
                     "strV0", "strV1", "strA0", "strA1", "strA2", "strL0", "strL1",
                     "swk0", "swk1", "swk2", "swk3", "sww", "swb",
                     "sy0", "sy1", "sy2", "sy3", "ssy"]
        semh = {n: e(nc.semaphore(n)) for n in sem_names}

        def dma(queue, out_ap, in_ap, sem, reads=(), writes=()):
            def fn(g, out_ap=out_ap, in_ap=in_ap):
                return g.dma_start(out=out_ap, in_=in_ap)
            P.op(queue, fn, reads=reads, writes=writes, sem=sem, inc=16)

        def dma_tr(out_ap, in_ap, sem, reads=(), writes=()):
            def fn(g, out_ap=out_ap, in_ap=in_ap):
                return g.dma_start_transpose(out=out_ap, in_=in_ap)
            P.op("sync", fn, reads=reads, writes=writes, sem=sem, inc=16)

        def init_loads(g):
            for o in range(HC):
                g.dma_start(out=bvec[:, o:o + 1], in_=b_c[o]).then_inc(semh["swb"], 16)
            g.dma_start(out=scrow[:], in_=sc_r).then_inc(semh["swb"], 16)
            g.dma_start(out=fbrow[:], in_=fb_r).then_inc(semh["swb"], 16)
            g.dma_start(out=ids[:], in_=id_h[:]).then_inc(semh["swb"], 16)
            return None
        P.op("sync", init_loads, writes=("bvec", "scrow", "fbrow", "ids"),
             sem="swb", inc=16 * (HC + 3), self_inc=False)

        P.op("vector", lambda v: v.memset(epsc[:], EPS), writes=("epsc",))
        P.op("scalar", lambda sce: sce.activation(out=scs[:, 3:4], in_=epsc[:],
                                                  func=AF.Ln, bias=epsc[:]),
             reads=("epsc",), writes=("warm",))
        P.op("vector", lambda v: v.memset(ones32[:], 1.0), writes=("ones32",))
        P.op("vector", lambda v: v.memset(onesbf[:], 1.0), writes=("onesbf",))
        P.op("tensor", lambda t: t.matmul(p_cx[:, 0:2], ones32[:], scrow[:],
                                          start=True, stop=True),
             reads=("ones32", "scrow"), writes=("pb4",))
        P.op("vector", lambda v: v.tensor_copy(out=scs[:, 4:6], in_=p_cx[:, 0:2]),
             reads=("pb4",), writes=("scs",))
        P.op("vector", lambda v: v.reciprocal(out=scs[:, 0:1], in_=scs[:, 4:5]),
             reads=("scs",), writes=("scs",))
        P.op("vector", lambda v: v.reciprocal(out=scs[:, 1:2], in_=scs[:, 5:6]),
             reads=("scs",), writes=("scs",))
        P.op("vector", lambda v: v.tensor_tensor(out=scs[:, 2:3], in0=scs[:, 4:5],
                                                 in1=scs[:, 5:6], op=ALU.mult),
             reads=("scs",), writes=("scs",))
        def fb_mm(t):
            t.matmul(p_ff[:, 0:512], onesbf[:, 0:128], fbrow[:, 0:512], start=True, stop=True)
            return t.matmul(p_ff[:, 512:1024], onesbf[:, 0:128], fbrow[:, 512:1024],
                            start=True, stop=True)
        P.op("tensor", fb_mm, reads=("onesbf", "fbrow"), writes=("pb6", "pb7"))

        def plan_fbb_evac():
            P.op("scalar", lambda sce: sce.activation(out=fbb[:], in_=p_ff[:, 0:1024],
                                                      func=AF.Copy),
                 reads=("pb6", "pb7"), writes=("fbb",))

        def plan_kq(j):
            sl = j % 2
            dma("scalar", kst[:, sl], ker_c[:, 2 * j:2 * j + 2, :], f"swk{sl}",
                writes=(f"kst{sl}",))
        def plan_kquant(j, h):
            def qfn(sce, j=j, h=h):
                return sce.activation(out=kq[:, 2 * j + h, :], in_=kst[:, j % 2, h],
                                      func=AF.Copy, scale=scs[:, 1:2])
            P.op("scalar", qfn, reads=(f"kst{j % 2}", "scs"), writes=("kq",))

        def ln_stats(slot, src_ap, stt, mvt, rsdt, keys):
            xkey, stkey, mvkey, rsdkey = keys
            P.op("vector", lambda v: v.bn_stats(out=stt[:, slot, 0], in_=src_ap[:, 0:512]),
                 reads=(xkey,), writes=(stkey,))
            P.op("vector", lambda v: v.bn_stats(out=stt[:, slot, 1], in_=src_ap[:, 512:1024]),
                 reads=(xkey,), writes=(stkey,))
            P.op("vector", lambda v: v.bn_aggr(out=mvt[:, slot], in_=stt[:, slot]),
                 reads=(stkey,), writes=(mvkey,))
            P.op("scalar", lambda sce: sce.activation(out=rsdt[:, slot, 0:1],
                                                      in_=mvt[:, slot, 1:2],
                                                      func=AF.Ln, bias=epsc[:]),
                 reads=(mvkey, "epsc"), writes=(rsdkey,))
            P.op("scalar", lambda sce: sce.activation(out=rsdt[:, slot, 1:2],
                                                      in_=rsdt[:, slot, 0:1],
                                                      func=AF.Exp, scale=-0.5),
                 reads=(rsdkey,), writes=(rsdkey,))

        NLD = 4

        def plan_A_load(s, t):
            slot = t % 4
            dma("gpsimd", xa[:, slot], x_t[s, t], f"sxa{slot}", writes=(f"xa{slot}",))

        for s in range(B):
            have_stats = s >= 1
            VA = 1 if have_stats else 2
            for t in range(T + VA + 6):
                if t < T and not (have_stats and t < NLD):
                    slot = t % 4
                    if slot % 2 == 0 and not (have_stats and t in (4, 6)):
                        dma("gpsimd", xa[:, slot], x_t[s, t], f"sxa{slot}",
                            writes=(f"xa{slot}",))
                    else:
                        dma("sync", xa[:, slot], x_t[s, t], f"sy{slot}",
                            writes=(f"xa{slot}",))
                if s == 0 and t < 4:
                    dma("scalar", WTs[:, 2 * t:2 * t + 2, :], wt_c[:, 2 * t:2 * t + 2, :],
                        "sww", writes=("WTs",))
                if VA <= t < T + VA:
                    ta = t - VA
                    slot = ta % 4
                    if have_stats:
                        P.op("vector", lambda v, ta=ta, slot=slot: v.tensor_scalar(
                                out=Vb[:, ta, :], in0=xa[:, slot],
                                scalar1=mv1b[:, ta, 0:1], scalar2=rsd1b[:, ta, 1:2],
                                op0=ALU.subtract, op1=ALU.mult),
                             reads=(f"xa{slot}", f"rsd1b{ta}", f"mv1b{ta}"),
                             writes=(f"Vb{ta}",))
                    else:
                        P.op("vector", lambda v, ta=ta, slot=slot: v.tensor_scalar(
                                out=Vb[:, ta, :], in0=xa[:, slot],
                                scalar1=mv1[:, slot, 0:1], scalar2=rsd1[:, slot, 1:2],
                                op0=ALU.subtract, op1=ALU.mult),
                             reads=(f"xa{slot}", f"rsd1{slot}", f"mv1{slot}"), writes=(f"Vb{ta}",))
                if t >= VA + 1:
                    tv = t - VA - 1
                    if tv < T + 4:
                        q = tv // 4
                        r = tv % 4
                        if tv < T:
                            pp = p_cx if tv % 2 == 0 else p_ff
                            pk = ("pb4", "pb5") if tv % 2 == 0 else ("pb6", "pb7")
                            def vtr(te, tv=tv, pp=pp):
                                last = None
                                for hh in range(HC):
                                    last = te.matmul(pp[:, hh * 128:(hh + 1) * 128],
                                                     Vb[:, tv, hh * 128:(hh + 1) * 128],
                                                     ids[:], start=True, stop=True)
                                return last
                            P.op("tensor", vtr, reads=(f"Vb{tv}", "ids"), writes=pk)
                            P.op("scalar", lambda sce, q=q, r=r, pp=pp: sce.activation(
                                    out=lnT[:, q % 2, :, r * 128:(r + 1) * 128],
                                    in_=pp[:, 0:1024], func=AF.Copy),
                                 reads=pk, writes=(f"lnT{q % 2}_{r}",))
                        qprev = tv // 4 - 1 + (1 if tv % 4 == 3 else 0)
                        if tv % 4 == 3:
                            qprev = None
                        qp_q = (tv - 4) // 4
                        if qp_q >= 0:
                            rr = tv % 4
                            if qp_q == 0 or qp_q == T // 4 - 1:
                                olist = (0, 1, 2, 3) if rr == 0 else (
                                    (4, 5, 6, 7) if rr == 1 else ())
                            else:
                                olist = (2 * rr, 2 * rr + 1)
                            for o in olist:
                                def qp(te, o=o, q=qp_q):
                                    last = None
                                    for hh in range(HC):
                                        last = te.matmul(BANK[o % 4],
                                                         WTs[:, hh, o * 128:(o + 1) * 128],
                                                         lnT[:, q % 2, hh, :],
                                                         start=(hh == 0), stop=(hh == HC - 1))
                                    return last
                                P.op("tensor", qp,
                                     reads=tuple(f"lnT{qp_q % 2}_{rr2}" for rr2 in range(4))
                                           + ("WTs",),
                                     writes=(f"pb{o % 4}",))
                                P.op("scalar", lambda sce, o=o, q=qp_q: sce.activation(
                                        out=qT[:, o, q * 512:(q + 1) * 512], in_=BANK[o % 4],
                                        func=AF.Identity, bias=bvec[:, o:o + 1]),
                                     reads=(f"pb{o % 4}", "bvec"), writes=(f"qT{qp_q}",))
                if t < T and not have_stats:
                    slot = t % 4
                    ln_stats(slot, xa[:, slot], st1, mv1, rsd1,
                             (f"xa{slot}", f"st1{slot}", f"mv1{slot}", f"rsd1{slot}"))
                if s == 0:
                    if t == 1:
                        plan_fbb_evac()
                    if t in (1, 3, 7, 11):
                        plan_kq({1: 0, 3: 1, 7: 2, 11: 3}[t])
                    KQS = {5: [(0, 0)], 6: [(0, 1)], 9: [(1, 0)], 10: [(1, 1)],
                           13: [(2, 0)], 14: [(2, 1)]}
                    t3a = 16 if T + VA + 6 > 17 else T + VA + 4
                    t3b = 17 if T + VA + 6 > 17 else T + VA + 5
                    KQS.setdefault(t3a, []).append((3, 0))
                    KQS.setdefault(t3b, []).append((3, 1))
                    for jh in KQS.get(t, ()):
                        plan_kquant(*jh)

            l2tr_pending = None
            pref_rsd_pending = None
            for it in range(T + 4):
                l2tr_flush, l2tr_pending = l2tr_pending, None
                pref_flush, pref_rsd_pending = pref_rsd_pending, None
                if s + 1 < B and T - 1 <= it < T - 1 + NLD:
                    plan_A_load(s + 1, it - (T - 1))

                def emit_stage1(it=it):
                    if it >= T:
                        return
                    t0 = it
                    sl0 = t0 % 2
                    sa0 = t0 % 3
                    sx0 = t0 % 4
                    bko = 4 if t0 == 1 else 0
                    for c in range(SC_CH):
                        def sc_mm(te, t0=t0, c=c, bko=bko):
                            last = None
                            for hh in range(HC):
                                last = te.matmul(BANK[bko + c],
                                                 qT[:, hh, t0 * 128:(t0 + 1) * 128],
                                                 qT[:, hh, c * 512:(c + 1) * 512],
                                                 start=(hh == 0), stop=(hh == HC - 1))
                            return last
                        P.op("tensor", sc_mm,
                             reads=(f"qT{c}", f"qT{t0 // 4}"), writes=(f"pb{bko + c}",))
                        P.op("vector", lambda v, sl0=sl0, c=c, bko=bko: v.reduce_max(
                                out=ngm4[:, sl0, c:c + 1], in_=BANK[bko + c], axis=AX.X),
                             reads=(f"pb{bko + c}",), writes=(f"ngm4{sl0}",))
                    P.op("vector", lambda v, sl0=sl0: v.reduce_max(
                            out=ngm[:, sl0], in_=ngm4[:, sl0, 0:SC_CH], axis=AX.X,
                            negate=True),
                         reads=(f"ngm4{sl0}",), writes=(f"ngm{sl0}",))
                    if bko == 4 and SC_CH == 4:
                        P.op("scalar", lambda sce, sl0=sl0: sce.activation(
                                out=attn[:, sl0, 0:1024], in_=p_cx[:, 0:1024],
                                func=AF.Exp, bias=ngm[:, sl0],
                                accum_out=zz[:, sa0, 0:1]),
                             reads=("pb4", "pb5", f"ngm{sl0}"),
                             writes=(f"attn{sl0}", f"zz{sa0}"))
                        P.op("scalar", lambda sce, sl0=sl0: sce.activation(
                                out=attn[:, sl0, 1024:2048], in_=p_ff[:, 0:1024],
                                func=AF.Exp, bias=ngm[:, sl0],
                                accum_out=zz[:, sa0, 1:2]),
                             reads=("pb6", "pb7", f"ngm{sl0}"),
                             writes=(f"attn{sl0}", f"zzb{sa0}"))
                        P.op("vector", lambda v, sa0=sa0: v.tensor_tensor(
                                out=zz[:, sa0, 0:1], in0=zz[:, sa0, 0:1],
                                in1=zz[:, sa0, 1:2], op=ALU.add),
                             reads=(f"zz{sa0}", f"zzb{sa0}"), writes=(f"zz{sa0}",))
                    elif bko == 4:
                        P.op("scalar", lambda sce, sl0=sl0: sce.activation(
                                out=attn[:, sl0], in_=p_cx[:, 0:S], func=AF.Exp,
                                bias=ngm[:, sl0], accum_out=zz[:, sa0, 0:1]),
                             reads=tuple(f"pb{4 + c}" for c in range(SC_CH))
                                   + (f"ngm{sl0}",),
                             writes=(f"attn{sl0}", f"zz{sa0}"))
                    else:
                        P.op("scalar", lambda sce, sl0=sl0: sce.activation(
                                out=attn[:, sl0], in_=p_sc[:, 0:S], func=AF.Exp,
                                bias=ngm[:, sl0], accum_out=zz[:, sa0, 0:1]),
                             reads=tuple(f"pb{c}" for c in range(SC_CH)) + (f"ngm{sl0}",),
                             writes=(f"attn{sl0}", f"zz{sa0}"))
                    dma_tr(attnT[:, sa0], attn[:, sl0], f"strA{sa0}",
                           reads=(f"attn{sl0}",), writes=(f"attnT{sa0}",))
                    dma("gpsimd", xa[:, sx0], x_t[s, t0], f"sxa{sx0}", writes=(f"xa{sx0}",))

                def emit_stage2(it=it):
                    nonlocal l2tr_pending
                    if not (2 <= it <= T + 1):
                        return
                    t1 = it - 2
                    sl1 = t1 % 2
                    sa1 = t1 % 3
                    sx1 = t1 % 4
                    sq1 = t1 % 4
                    pcx, pck = p_cx[:, 0:1024], ("pb4", "pb5")
                    def cx_half(hf):
                        def fn(te, sa1=sa1, pcx=pcx, hf=hf):
                            last = None
                            for c in range(T):
                                last = te.matmul(pcx[:, hf * 512:hf * 512 + 512],
                                                 attnT[:, sa1, c, :],
                                                 Vb[:, c, hf * 512:hf * 512 + 512],
                                                 start=(c == 0), stop=(c == T - 1))
                            return last
                        return fn
                    P.op("vector", lambda v, sl1=sl1, sa1=sa1: v.reciprocal(
                            out=rz[:, sl1], in_=zz[:, sa1, 0:1]),
                         reads=(f"zz{sa1}",), writes=(f"rz{sl1}",))
                    if t1 >= T - 4:
                        P.op("tensor", cx_half(0),
                             reads=(f"attnT{sa1}",) + tuple(f"Vb{c}" for c in range(T)),
                             writes=(pck[0],))
                        P.op("tensor", cx_half(1),
                             reads=(f"attnT{sa1}",) + tuple(f"Vb{c}" for c in range(T)),
                             writes=(pck[1],))
                        for hf in range(2):
                            P.op("vector", lambda v, sl1=sl1, sx1=sx1, sq1=sq1,
                                 pcx=pcx, hf=hf: v.scalar_tensor_tensor(
                                    out=x2[:, sq1, hf * 512:hf * 512 + 512],
                                    in0=pcx[:, hf * 512:hf * 512 + 512],
                                    scalar=rz[:, sl1],
                                    in1=xa[:, sx1, hf * 512:hf * 512 + 512],
                                    op0=ALU.mult, op1=ALU.add),
                                 reads=(pck[hf], f"rz{sl1}", f"xa{sx1}"),
                                 writes=(f"x2{sq1}h{hf}", f"x2{sq1}"))
                            P.op("vector", lambda v, sl1=sl1, sq1=sq1, hf=hf: v.bn_stats(
                                    out=st2[:, sl1, hf],
                                    in_=x2[:, sq1, hf * 512:hf * 512 + 512]),
                                 reads=(f"x2{sq1}h{hf}",), writes=(f"st2{sl1}",))
                        P.op("vector", lambda v, sl1=sl1: v.bn_aggr(
                                out=mv2[:, sl1], in_=st2[:, sl1]),
                             reads=(f"st2{sl1}",), writes=(f"mv2{sl1}",))
                        P.op("scalar", lambda sce, sl1=sl1: sce.activation(
                                out=rsd2[:, sl1, 0:1], in_=mv2[:, sl1, 1:2],
                                func=AF.Ln, bias=epsc[:]),
                             reads=(f"mv2{sl1}", "epsc"), writes=(f"rsd2{sl1}",))
                        P.op("scalar", lambda sce, sl1=sl1: sce.activation(
                                out=rsd2[:, sl1, 1:2], in_=rsd2[:, sl1, 0:1],
                                func=AF.Exp, scale=-0.5),
                             reads=(f"rsd2{sl1}",), writes=(f"rsd2{sl1}",))
                    else:
                        P.op("tensor", lambda te, sa1=sa1, pcx=pcx: (
                                cx_half(0)(te, sa1=sa1, pcx=pcx),
                                cx_half(1)(te, sa1=sa1, pcx=pcx))[-1],
                             reads=(f"attnT{sa1}",) + tuple(f"Vb{c}" for c in range(T)),
                             writes=pck)
                        P.op("vector", lambda v, sl1=sl1, sx1=sx1, sq1=sq1, pcx=pcx: v.scalar_tensor_tensor(
                                out=x2[:, sq1], in0=pcx[:, 0:1024], scalar=rz[:, sl1],
                                in1=xa[:, sx1], op0=ALU.mult, op1=ALU.add),
                             reads=pck + (f"rz{sl1}", f"xa{sx1}"),
                             writes=(f"x2{sq1}",))
                        ln_stats(sl1, x2[:, sq1], st2, mv2, rsd2,
                                 (f"x2{sq1}", f"st2{sl1}", f"mv2{sl1}", f"rsd2{sl1}"))
                    P.op("vector", lambda v, sl1=sl1, sq1=sq1: v.tensor_scalar(
                            out=ln2[:, sl1], in0=x2[:, sq1],
                            scalar1=mv2[:, sl1, 0:1], scalar2=rsd2[:, sl1, 1:2],
                            op0=ALU.subtract, op1=ALU.mult),
                         reads=(f"x2{sq1}", f"rsd2{sl1}", f"mv2{sl1}"),
                         writes=(f"ln2{sl1}",))
                    if t1 >= T - 4:
                        pbq = p_sc[:, 0:1024] if t1 % 2 == 0 else p_sc[:, 1024:2048]
                        pkq = ("pb0", "pb1") if t1 % 2 == 0 else ("pb2", "pb3")
                        def emit_l2tr(sl1=sl1, sq1=sq1, pbq=pbq, pkq=pkq,
                                      split=(t1 >= T - 2)):
                            def l2tr(te, sl1=sl1, pbq=pbq):
                                last = None
                                for hh in range(HC):
                                    last = te.matmul(pbq[:, hh * 128:(hh + 1) * 128],
                                                     ln2[:, sl1, hh * 128:(hh + 1) * 128],
                                                     ids[:], start=True, stop=True)
                                return last
                            P.op("tensor", l2tr, reads=(f"ln2{sl1}", "ids"), writes=pkq)
                            if split:
                                for hf in range(2):
                                    P.op("scalar", lambda sce, sq1=sq1, pbq=pbq,
                                         hf=hf: sce.activation(
                                            out=ln2q[:, sq1, 4 * hf:4 * hf + 4, :],
                                            in_=pbq[:, hf * 512:hf * 512 + 512],
                                            func=AF.Copy, scale=scs[:, 0:1]),
                                         reads=(pkq[hf], "scs"),
                                         writes=(f"ln2q{sq1}h{hf}", f"ln2q{sq1}"))
                            else:
                                P.op("scalar", lambda sce, sq1=sq1, pbq=pbq: sce.activation(
                                        out=ln2q[:, sq1], in_=pbq[:, 0:1024], func=AF.Copy,
                                        scale=scs[:, 0:1]),
                                     reads=pkq + ("scs",),
                                     writes=(f"ln2q{sq1}", f"ln2q{sq1}h0",
                                             f"ln2q{sq1}h1"))
                        l2tr_pending = emit_l2tr
                    else:
                        dma_tr(ln2T[:, sl1], ln2[:, sl1], f"strL{sl1}",
                               reads=(f"ln2{sl1}",), writes=(f"ln2T{sl1}",))
                        P.op("scalar", lambda sce, sl1=sl1, sq1=sq1: sce.activation(
                                out=ln2q[:, sq1], in_=ln2T[:, sl1], func=AF.Copy,
                                scale=scs[:, 0:1]),
                             reads=(f"ln2T{sl1}", "scs"),
                             writes=(f"ln2q{sq1}", f"ln2q{sq1}h0", f"ln2q{sq1}h1"))

                emit_stage1()
                emit_stage2()

                if l2tr_flush is not None and it <= T + 1:
                    l2tr_flush()
                    l2tr_flush = None

                if s + 1 < B:
                    kl = it
                    if kl < T:
                        dma("scalar", xb[:, kl % 2], x_t[s + 1, kl], f"sxb{kl % 2}",
                            writes=(f"xb{kl % 2}",))
                    ks = it - 1
                    if 0 <= ks < T:
                        kslot = ks % 2
                        P.op("vector", lambda v, ks=ks, kslot=kslot: v.bn_stats(
                                out=st1b[:, kslot, 0], in_=xb[:, kslot, 0:512]),
                             reads=(f"xb{kslot}",), writes=(f"st1b{kslot}",))
                        P.op("vector", lambda v, ks=ks, kslot=kslot: v.bn_stats(
                                out=st1b[:, kslot, 1], in_=xb[:, kslot, 512:1024]),
                             reads=(f"xb{kslot}",), writes=(f"st1b{kslot}",))
                        P.op("vector", lambda v, ks=ks, kslot=kslot: v.bn_aggr(
                                out=mv1b[:, ks], in_=st1b[:, kslot]),
                             reads=(f"st1b{kslot}",), writes=(f"mv1b{ks}",))
                        def emit_pref_rsd(ks=ks):
                            P.op("scalar", lambda sce, ks=ks: sce.activation(
                                    out=rsd1b[:, ks, 0:1], in_=mv1b[:, ks, 1:2],
                                    func=AF.Ln, bias=epsc[:]),
                                 reads=(f"mv1b{ks}", "epsc"), writes=(f"rsd1b{ks}",))
                            P.op("scalar", lambda sce, ks=ks: sce.activation(
                                    out=rsd1b[:, ks, 1:2], in_=rsd1b[:, ks, 0:1],
                                    func=AF.Exp, scale=-0.5),
                                 reads=(f"rsd1b{ks}",), writes=(f"rsd1b{ks}",))
                        pref_rsd_pending = emit_pref_rsd

                if 4 <= it <= T + 3:
                    t2 = it - 4
                    sl2 = t2 % 2
                    sq2 = t2 % 4
                    DR = mybir.MatmulPerfMode.DoubleRow
                    def ff_mm(te, sq2=sq2):
                        last = None
                        for j in range(HC // 2):
                            last = te.matmul(p_ff[:, 0:512], ln2q[:, sq2, 2 * j:2 * j + 2, :],
                                             kq[:, 2 * j:2 * j + 2, 0:512],
                                             start=(j == 0), stop=(j == HC // 2 - 1),
                                             perf_mode=DR)
                        for j in range(HC // 2):
                            last = te.matmul(p_ff[:, 512:1024], ln2q[:, sq2, 2 * j:2 * j + 2, :],
                                             kq[:, 2 * j:2 * j + 2, 512:1024],
                                             start=(j == 0), stop=(j == HC // 2 - 1),
                                             perf_mode=DR)
                        return last
                    if t2 >= T - 2:
                        for hf in range(2):
                            def ff_h(te, sq2=sq2, hf=hf):
                                last = None
                                for j in (2 * hf, 2 * hf + 1):
                                    last = te.matmul(p_ff[:, 0:512],
                                                     ln2q[:, sq2, 2 * j:2 * j + 2, :],
                                                     kq[:, 2 * j:2 * j + 2, 0:512],
                                                     start=(j == 0),
                                                     stop=(j == HC // 2 - 1),
                                                     perf_mode=DR)
                                for j in (2 * hf, 2 * hf + 1):
                                    last = te.matmul(p_ff[:, 512:1024],
                                                     ln2q[:, sq2, 2 * j:2 * j + 2, :],
                                                     kq[:, 2 * j:2 * j + 2, 512:1024],
                                                     start=(j == 0),
                                                     stop=(j == HC // 2 - 1),
                                                     perf_mode=DR)
                                return last
                            P.op("tensor", ff_h, reads=(f"ln2q{sq2}h{hf}", "kq"),
                                 writes=("pb6", "pb7"))
                    else:
                        P.op("tensor", ff_mm, reads=(f"ln2q{sq2}", "kq"),
                             writes=("pb6", "pb7"))
                    P.op("scalar", lambda sce, sl2=sl2: sce.activation(
                            out=y1[:, sl2], in_=p_ff[:, 0:1024], func=AF.Copy,
                            scale=scs[:, 2:3]),
                         reads=("pb6", "pb7", "scs"), writes=(f"y1{sl2}",))
                    if s == B - 1 and t2 >= T - 2:
                        P.op("vector", lambda v, sl2=sl2: v.tensor_tensor(
                                out=yy[:, sl2], in0=y1[:, sl2], in1=fbb[:],
                                op=ALU.add),
                             reads=(f"y1{sl2}", "fbb"), writes=(f"yy{sl2}",))
                        for h in range(2):
                            P.op("vector", lambda v, sl2=sl2, sq2=sq2, h=h:
                                 v.tensor_tensor(
                                    out=xo[:, sl2, h * 512:(h + 1) * 512],
                                    in0=x2[:, sq2, h * 512:(h + 1) * 512],
                                    in1=yy[:, sl2, h * 512:(h + 1) * 512],
                                    op=ALU.add),
                                 reads=(f"x2{sq2}", f"yy{sl2}"),
                                 writes=(f"xo{sl2}h{h}", f"xo{sl2}"))
                        dma("gpsimd", y_t[s, t2][:, 0:512], xo[:, sl2, 0:512],
                            f"sst{sl2}", reads=(f"xo{sl2}h0",))
                        dma("sync", y_t[s, t2][:, 512:1024], xo[:, sl2, 512:1024],
                            "ssy", reads=(f"xo{sl2}h1",))
                    else:
                        P.op("gpsimd", lambda g, sl2=sl2: g.tensor_tensor(
                                out=yy[:, sl2], in0=y1[:, sl2], in1=fbb[:], op=ALU.add),
                             reads=(f"y1{sl2}", "fbb"), writes=(f"yy{sl2}",))
                        P.op("gpsimd", lambda g, sl2=sl2, sq2=sq2: g.tensor_tensor(
                                out=xo[:, sl2], in0=x2[:, sq2], in1=yy[:, sl2], op=ALU.add),
                             reads=(f"x2{sq2}", f"yy{sl2}"), writes=(f"xo{sl2}",))
                        dma("gpsimd", y_t[s, t2], xo[:, sl2], f"sst{sl2}",
                            reads=(f"xo{sl2}",))

                if pref_flush is not None:
                    pref_flush()

                if l2tr_flush is not None:
                    l2tr_flush()

        P.wait("gpsimd", "sst0", P.semcnt["sst0"])
        P.wait("gpsimd", "sst1", P.semcnt["sst1"])
        P.wait("gpsimd", "ssy", P.semcnt["ssy"])

        with nc.Block() as block:
            def mk(eng_name):
                def run(eng):
                    for item in P.streams[eng_name]:
                        if item[0] == "w":
                            eng.wait_ge(semh[item[1]], item[2])
                        else:
                            _, fn, semn, inc, self_inc = item
                            instr = fn(eng)
                            if self_inc:
                                instr.then_inc(semh[semn], inc)
                return run
            block.sync(mk("sync"))
            block.vector(mk("vector"))
            block.scalar(mk("scalar"))
            block.tensor(mk("tensor"))
            block.gpsimd(mk("gpsimd"))

    return nc


def kernel(L0_attn_lstm_out, L0_attn_W, L0_attn_b, L0_ffn_kernel,
           L0_ffn_input_scale, L0_ffn_kernel_scale, L0_ffn_bias):
    bf16 = ml_dtypes.bfloat16
    x = np.ascontiguousarray(L0_attn_lstm_out, dtype=np.float32)
    WT = np.ascontiguousarray(np.asarray(L0_attn_W, dtype=np.float32).T.astype(bf16))
    b = np.ascontiguousarray(L0_attn_b, dtype=np.float32)
    ker = np.ascontiguousarray(np.asarray(L0_ffn_kernel).astype(bf16))
    sc = np.array([np.float32(L0_ffn_input_scale[0]),
                   np.float32(L0_ffn_kernel_scale[0])], dtype=np.float32)
    fb = np.ascontiguousarray(np.asarray(L0_ffn_bias).astype(bf16))
    ids = np.eye(128, dtype=bf16)

    BATCH, S, _ = x.shape
    per = BATCH // N_CORES
    nc = build_nc(per, S)
    in_maps = []
    for i in range(N_CORES):
        in_maps.append({
            "x": np.ascontiguousarray(x[i * per:(i + 1) * per]),
            "wt": WT, "b": b, "ker": ker, "sc": sc, "fb": fb, "ids": ids,
        })
    res = run_bass_kernel_spmd(nc, in_maps, core_ids=list(range(N_CORES)))
    out = np.concatenate([np.asarray(r["y"]) for r in res.results], axis=0)
    return out.astype(np.float32)


# version 81 (final)
# speedup vs baseline: 1.0283x; 1.0050x over previous
import contextlib
from collections import defaultdict

import numpy as np
import ml_dtypes

import concourse.bass as bass
import concourse.mybir as mybir
from concourse.bass_utils import run_bass_kernel_spmd
from concourse.alu_op_type import AluOpType as ALU

F32 = mybir.dt.float32
BF16 = mybir.dt.bfloat16
F8 = mybir.dt.float8e4
AF = mybir.ActivationFunctionType
AX = mybir.AxisListType

N_CORES = 8
H = 1024
HC = H // 128
EPS = 1e-5

ENGINES = ("sync", "vector", "scalar", "tensor", "gpsimd")
ENG_SEM = {"vector": "sv", "scalar": "sa", "tensor": "sp", "gpsimd": "sgc"}


class Plan:

    def __init__(self):
        self.streams = {e: [] for e in ENGINES}
        self.semcnt = defaultdict(int)
        self.hw = {e: defaultdict(int) for e in ENGINES}
        self.bufw = {}
        self.bufr = defaultdict(list)

    def wait(self, eng, sem, cnt):
        if cnt <= self.hw[eng][sem]:
            return
        self.hw[eng][sem] = cnt
        self.streams[eng].append(("w", sem, cnt))

    def op(self, eng, fn, reads=(), writes=(), sem=None, inc=1, self_inc=True):
        need = {}
        def _need(sem2, cnt2):
            if cnt2 > need.get(sem2, -1):
                need[sem2] = cnt2
        for b in reads:
            lw = self.bufw.get(b)
            if lw:
                _need(*lw)
        for b in writes:
            lw = self.bufw.get(b)
            if lw:
                _need(*lw)
            for r in self.bufr.get(b, ()):
                _need(*r)
        for sem2, cnt2 in need.items():
            self.wait(eng, sem2, cnt2)
        s = sem if sem is not None else ENG_SEM[eng]
        self.semcnt[s] += inc
        c = self.semcnt[s]
        self.streams[eng].append(("o", fn, s, inc, self_inc))
        for b in reads:
            self.bufr[b].append((s, c))
        for b in writes:
            self.bufw[b] = (s, c)
            self.bufr[b] = []
        return (s, c)


def build_nc(B, S):
    T = S // 128
    NQ = T // 4
    SC_CH = S // 512

    nc = bass.Bass("TRN2", target_bir_lowering=False)

    x_h = nc.dram_tensor("x", [B, S, H], F32, kind="ExternalInput").ap()
    wt_h = nc.dram_tensor("wt", [H, H], BF16, kind="ExternalInput").ap()
    b_h = nc.dram_tensor("b", [H], F32, kind="ExternalInput").ap()
    ker_h = nc.dram_tensor("ker", [H, H], BF16, kind="ExternalInput").ap()
    sc_h = nc.dram_tensor("sc", [2], F32, kind="ExternalInput").ap()
    fb_h = nc.dram_tensor("fb", [H], BF16, kind="ExternalInput").ap()
    id_h = nc.dram_tensor("ids", [128, 128], BF16, kind="ExternalInput").ap()
    y_h = nc.dram_tensor("y", [B, S, H], F32, kind="ExternalOutput").ap()

    x_t = x_h.rearrange("s (t p) h -> s t p h", p=128)
    y_t = y_h.rearrange("s (t p) h -> s t p h", p=128)
    wt_c = wt_h.rearrange("(c p) o -> p c o", p=128)
    ker_c = ker_h.rearrange("(c p) o -> p c o", p=128)
    b_c = b_h.rearrange("(c p one) -> c p one", c=HC, one=1)
    sc_r = sc_h.rearrange("(one s) -> one s", one=1)
    fb_r = fb_h.rearrange("(one h) -> one h", one=1)

    P = Plan()

    with contextlib.ExitStack() as es:
        e = es.enter_context
        WTs = e(nc.sbuf_tensor("WTs", [128, HC, H], BF16))
        kst = e(nc.sbuf_tensor("kst", [128, 2, 2, H], BF16))
        kq = e(nc.sbuf_tensor("kq", [128, HC, H], F8))
        Vb = e(nc.sbuf_tensor("Vb", [128, T, H], BF16))
        qT = e(nc.sbuf_tensor("qT", [128, HC, S], BF16))
        lnT = e(nc.sbuf_tensor("lnT", [128, 2, HC, 512], BF16))
        xa = e(nc.sbuf_tensor("xa", [128, 4, H], F32))
        attn = e(nc.sbuf_tensor("attn", [128, 2, S], BF16))
        attnT = e(nc.sbuf_tensor("attnT", [128, 3, T, 128], BF16))
        x2 = e(nc.sbuf_tensor("x2", [128, 4, H], F32))
        ln2 = e(nc.sbuf_tensor("ln2", [128, 2, H], BF16))
        ln2T = e(nc.sbuf_tensor("ln2T", [128, 2, HC, 128], BF16))
        ln2q = e(nc.sbuf_tensor("ln2q", [128, 4, HC, 128], F8))
        y1 = e(nc.sbuf_tensor("y1", [128, 2, H], BF16))
        yy = e(nc.sbuf_tensor("yy", [128, 2, H], BF16))
        xo = e(nc.sbuf_tensor("xo", [128, 2, H], F32))
        fbb = e(nc.sbuf_tensor("fbb", [128, H], BF16))
        bvec = e(nc.sbuf_tensor("bvec", [128, HC], F32))
        scs = e(nc.sbuf_tensor("scs", [128, 6], F32))
        st1 = e(nc.sbuf_tensor("st1", [128, 4, 2, 6], F32))
        xb = e(nc.sbuf_tensor("xb", [128, 2, H], F32))
        st1b = e(nc.sbuf_tensor("st1b", [128, 2, 2, 6], F32))
        mv1b = e(nc.sbuf_tensor("mv1b", [128, T, 2], F32))
        rsd1b = e(nc.sbuf_tensor("rsd1b", [128, T, 2], F32))
        mv1 = e(nc.sbuf_tensor("mv1", [128, 4, 2], F32))
        rsd1 = e(nc.sbuf_tensor("rsd1", [128, 4, 2], F32))
        st2 = e(nc.sbuf_tensor("st2", [128, 2, 2, 6], F32))
        mv2 = e(nc.sbuf_tensor("mv2", [128, 2, 2], F32))
        rsd2 = e(nc.sbuf_tensor("rsd2", [128, 2, 2], F32))
        ngm = e(nc.sbuf_tensor("ngm", [128, 2, 1], F32))
        ngm4 = e(nc.sbuf_tensor("ngm4", [128, 2, 4], F32))
        zz = e(nc.sbuf_tensor("zz", [128, 3, 2], F32))
        rz = e(nc.sbuf_tensor("rz", [128, 2, 1], F32))
        epsc = e(nc.sbuf_tensor("epsc", [128, 1], F32))
        ones32 = e(nc.sbuf_tensor("ones32", [1, 128], F32))
        onesbf = e(nc.sbuf_tensor("onesbf", [1, 512], BF16))
        scrow = e(nc.sbuf_tensor("scrow", [1, 2], F32))
        fbrow = e(nc.sbuf_tensor("fbrow", [1, H], BF16))
        ids = e(nc.sbuf_tensor("ids_s", [128, 128], BF16))

        p_sc = e(nc.psum_tensor("p_sc", [128, 4 * 512], F32))
        p_cx = e(nc.psum_tensor("p_cx", [128, 2 * 512], F32))
        p_ff = e(nc.psum_tensor("p_ff", [128, 2 * 512], F32))
        BANK = ([p_sc[:, i * 512:(i + 1) * 512] for i in range(4)]
                + [p_cx[:, i * 512:(i + 1) * 512] for i in range(2)]
                + [p_ff[:, i * 512:(i + 1) * 512] for i in range(2)])

        sem_names = ["sv", "sa", "sp", "sgc",
                     "sxa0", "sxa1", "sxa2", "sxa3", "sxb0", "sxb1", "sst0", "sst1",
                     "strV0", "strV1", "strA0", "strA1", "strA2", "strL0", "strL1",
                     "swk0", "swk1", "swk2", "swk3", "sww", "swb",
                     "sy0", "sy1", "sy2", "sy3", "sz0", "sz1", "sz2", "sz3", "ssy"]
        semh = {n: e(nc.semaphore(n)) for n in sem_names}

        def dma(queue, out_ap, in_ap, sem, reads=(), writes=()):
            def fn(g, out_ap=out_ap, in_ap=in_ap):
                return g.dma_start(out=out_ap, in_=in_ap)
            P.op(queue, fn, reads=reads, writes=writes, sem=sem, inc=16)

        def dma_tr(out_ap, in_ap, sem, reads=(), writes=()):
            def fn(g, out_ap=out_ap, in_ap=in_ap):
                return g.dma_start_transpose(out=out_ap, in_=in_ap)
            P.op("sync", fn, reads=reads, writes=writes, sem=sem, inc=16)

        def init_loads(g):
            for o in range(HC):
                g.dma_start(out=bvec[:, o:o + 1], in_=b_c[o]).then_inc(semh["swb"], 16)
            g.dma_start(out=scrow[:], in_=sc_r).then_inc(semh["swb"], 16)
            g.dma_start(out=fbrow[:], in_=fb_r).then_inc(semh["swb"], 16)
            g.dma_start(out=ids[:], in_=id_h[:]).then_inc(semh["swb"], 16)
            return None
        P.op("sync", init_loads, writes=("bvec", "scrow", "fbrow", "ids"),
             sem="swb", inc=16 * (HC + 3), self_inc=False)

        P.op("vector", lambda v: v.memset(epsc[:], EPS), writes=("epsc",))
        P.op("scalar", lambda sce: sce.activation(out=scs[:, 3:4], in_=epsc[:],
                                                  func=AF.Ln, bias=epsc[:]),
             reads=("epsc",), writes=("warm",))
        P.op("vector", lambda v: v.memset(ones32[:], 1.0), writes=("ones32",))
        P.op("vector", lambda v: v.memset(onesbf[:], 1.0), writes=("onesbf",))
        P.op("tensor", lambda t: t.matmul(p_cx[:, 0:2], ones32[:], scrow[:],
                                          start=True, stop=True),
             reads=("ones32", "scrow"), writes=("pb4",))
        P.op("vector", lambda v: v.tensor_copy(out=scs[:, 4:6], in_=p_cx[:, 0:2]),
             reads=("pb4",), writes=("scs",))
        P.op("vector", lambda v: v.reciprocal(out=scs[:, 0:1], in_=scs[:, 4:5]),
             reads=("scs",), writes=("scs",))
        P.op("vector", lambda v: v.reciprocal(out=scs[:, 1:2], in_=scs[:, 5:6]),
             reads=("scs",), writes=("scs",))
        P.op("vector", lambda v: v.tensor_tensor(out=scs[:, 2:3], in0=scs[:, 4:5],
                                                 in1=scs[:, 5:6], op=ALU.mult),
             reads=("scs",), writes=("scs",))
        def fb_mm(t):
            t.matmul(p_ff[:, 0:512], onesbf[:, 0:128], fbrow[:, 0:512], start=True, stop=True)
            return t.matmul(p_ff[:, 512:1024], onesbf[:, 0:128], fbrow[:, 512:1024],
                            start=True, stop=True)
        P.op("tensor", fb_mm, reads=("onesbf", "fbrow"), writes=("pb6", "pb7"))

        def plan_fbb_evac():
            P.op("scalar", lambda sce: sce.activation(out=fbb[:], in_=p_ff[:, 0:1024],
                                                      func=AF.Copy),
                 reads=("pb6", "pb7"), writes=("fbb",))

        def plan_kq(j):
            sl = j % 2
            dma("scalar", kst[:, sl], ker_c[:, 2 * j:2 * j + 2, :], f"swk{sl}",
                writes=(f"kst{sl}",))
        def plan_kquant(j, h):
            def qfn(sce, j=j, h=h):
                return sce.activation(out=kq[:, 2 * j + h, :], in_=kst[:, j % 2, h],
                                      func=AF.Copy, scale=scs[:, 1:2])
            P.op("scalar", qfn, reads=(f"kst{j % 2}", "scs"), writes=("kq",))

        def ln_stats(slot, src_ap, stt, mvt, rsdt, keys):
            xkey, stkey, mvkey, rsdkey = keys
            P.op("vector", lambda v: v.bn_stats(out=stt[:, slot, 0], in_=src_ap[:, 0:512]),
                 reads=(xkey,), writes=(stkey,))
            P.op("vector", lambda v: v.bn_stats(out=stt[:, slot, 1], in_=src_ap[:, 512:1024]),
                 reads=(xkey,), writes=(stkey,))
            P.op("vector", lambda v: v.bn_aggr(out=mvt[:, slot], in_=stt[:, slot]),
                 reads=(stkey,), writes=(mvkey,))
            P.op("scalar", lambda sce: sce.activation(out=rsdt[:, slot, 0:1],
                                                      in_=mvt[:, slot, 1:2],
                                                      func=AF.Ln, bias=epsc[:]),
                 reads=(mvkey, "epsc"), writes=(rsdkey,))
            P.op("scalar", lambda sce: sce.activation(out=rsdt[:, slot, 1:2],
                                                      in_=rsdt[:, slot, 0:1],
                                                      func=AF.Exp, scale=-0.5),
                 reads=(rsdkey,), writes=(rsdkey,))

        NLD = 4

        def plan_A_load(s, t):
            slot = t % 4
            dma("gpsimd", xa[:, slot], x_t[s, t], f"sxa{slot}", writes=(f"xa{slot}",))

        for s in range(B):
            have_stats = s >= 1
            VA = 1 if have_stats else 2
            for t in range(T + VA + 6):
                if t < T and not (have_stats and t < NLD):
                    slot = t % 4
                    if slot % 2 == 0 and not (have_stats and t in (4, 6)):
                        dma("gpsimd", xa[:, slot], x_t[s, t], f"sxa{slot}",
                            writes=(f"xa{slot}",))
                    else:
                        dma("sync", xa[:, slot], x_t[s, t], f"sy{slot}",
                            writes=(f"xa{slot}",))
                if s == 0 and t < 4:
                    dma("scalar", WTs[:, 2 * t:2 * t + 2, :], wt_c[:, 2 * t:2 * t + 2, :],
                        "sww", writes=("WTs",))
                if VA <= t < T + VA:
                    ta = t - VA
                    slot = ta % 4
                    if have_stats:
                        P.op("vector", lambda v, ta=ta, slot=slot: v.tensor_scalar(
                                out=Vb[:, ta, :], in0=xa[:, slot],
                                scalar1=mv1b[:, ta, 0:1], scalar2=rsd1b[:, ta, 1:2],
                                op0=ALU.subtract, op1=ALU.mult),
                             reads=(f"xa{slot}", f"rsd1b{ta}", f"mv1b{ta}"),
                             writes=(f"Vb{ta}",))
                    else:
                        P.op("vector", lambda v, ta=ta, slot=slot: v.tensor_scalar(
                                out=Vb[:, ta, :], in0=xa[:, slot],
                                scalar1=mv1[:, slot, 0:1], scalar2=rsd1[:, slot, 1:2],
                                op0=ALU.subtract, op1=ALU.mult),
                             reads=(f"xa{slot}", f"rsd1{slot}", f"mv1{slot}"), writes=(f"Vb{ta}",))
                if t >= VA + 1:
                    tv = t - VA - 1
                    if tv < T + 4:
                        q = tv // 4
                        r = tv % 4
                        if tv < T:
                            pp = p_cx if tv % 2 == 0 else p_ff
                            pk = ("pb4", "pb5") if tv % 2 == 0 else ("pb6", "pb7")
                            def vtr(te, tv=tv, pp=pp):
                                last = None
                                for hh in range(HC):
                                    last = te.matmul(pp[:, hh * 128:(hh + 1) * 128],
                                                     Vb[:, tv, hh * 128:(hh + 1) * 128],
                                                     ids[:], start=True, stop=True)
                                return last
                            P.op("tensor", vtr, reads=(f"Vb{tv}", "ids"), writes=pk)
                            P.op("scalar", lambda sce, q=q, r=r, pp=pp: sce.activation(
                                    out=lnT[:, q % 2, :, r * 128:(r + 1) * 128],
                                    in_=pp[:, 0:1024], func=AF.Copy),
                                 reads=pk, writes=(f"lnT{q % 2}_{r}",))
                        qprev = tv // 4 - 1 + (1 if tv % 4 == 3 else 0)
                        if tv % 4 == 3:
                            qprev = None
                        qp_q = (tv - 4) // 4
                        if qp_q >= 0:
                            rr = tv % 4
                            if qp_q == 0 or qp_q == T // 4 - 1:
                                olist = (0, 1, 2, 3) if rr == 0 else (
                                    (4, 5, 6, 7) if rr == 1 else ())
                            else:
                                olist = (2 * rr, 2 * rr + 1)
                            for o in olist:
                                def qp(te, o=o, q=qp_q):
                                    last = None
                                    for hh in range(HC):
                                        last = te.matmul(BANK[o % 4],
                                                         WTs[:, hh, o * 128:(o + 1) * 128],
                                                         lnT[:, q % 2, hh, :],
                                                         start=(hh == 0), stop=(hh == HC - 1))
                                    return last
                                P.op("tensor", qp,
                                     reads=tuple(f"lnT{qp_q % 2}_{rr2}" for rr2 in range(4))
                                           + ("WTs",),
                                     writes=(f"pb{o % 4}",))
                                P.op("scalar", lambda sce, o=o, q=qp_q: sce.activation(
                                        out=qT[:, o, q * 512:(q + 1) * 512], in_=BANK[o % 4],
                                        func=AF.Identity, bias=bvec[:, o:o + 1]),
                                     reads=(f"pb{o % 4}", "bvec"), writes=(f"qT{qp_q}",))
                if t < T and not have_stats:
                    slot = t % 4
                    ln_stats(slot, xa[:, slot], st1, mv1, rsd1,
                             (f"xa{slot}", f"st1{slot}", f"mv1{slot}", f"rsd1{slot}"))
                if s == 0:
                    if t == 1:
                        plan_fbb_evac()
                    if t in (1, 3, 7, 11):
                        plan_kq({1: 0, 3: 1, 7: 2, 11: 3}[t])
                    KQS = {5: [(0, 0)], 6: [(0, 1)], 9: [(1, 0)], 10: [(1, 1)],
                           13: [(2, 0)], 14: [(2, 1)]}
                    t3a = 16 if T + VA + 6 > 17 else T + VA + 4
                    t3b = 17 if T + VA + 6 > 17 else T + VA + 5
                    KQS.setdefault(t3a, []).append((3, 0))
                    KQS.setdefault(t3b, []).append((3, 1))
                    for jh in KQS.get(t, ()):
                        plan_kquant(*jh)

            l2tr_pending = None
            pref_rsd_pending = None
            for it in range(T + 4):
                l2tr_flush, l2tr_pending = l2tr_pending, None
                pref_flush, pref_rsd_pending = pref_rsd_pending, None
                if s + 1 < B and T - 1 <= it < T - 1 + NLD:
                    plan_A_load(s + 1, it - (T - 1))

                def emit_stage1(it=it):
                    if it >= T:
                        return
                    t0 = it
                    sl0 = t0 % 2
                    sa0 = t0 % 3
                    sx0 = t0 % 4
                    bko = 4 if t0 == 1 else 0
                    for c in range(SC_CH):
                        def sc_mm(te, t0=t0, c=c, bko=bko):
                            last = None
                            for hh in range(HC):
                                last = te.matmul(BANK[bko + c],
                                                 qT[:, hh, t0 * 128:(t0 + 1) * 128],
                                                 qT[:, hh, c * 512:(c + 1) * 512],
                                                 start=(hh == 0), stop=(hh == HC - 1))
                            return last
                        P.op("tensor", sc_mm,
                             reads=(f"qT{c}", f"qT{t0 // 4}"), writes=(f"pb{bko + c}",))
                        P.op("vector", lambda v, sl0=sl0, c=c, bko=bko: v.reduce_max(
                                out=ngm4[:, sl0, c:c + 1], in_=BANK[bko + c], axis=AX.X),
                             reads=(f"pb{bko + c}",), writes=(f"ngm4{sl0}",))
                    P.op("vector", lambda v, sl0=sl0: v.reduce_max(
                            out=ngm[:, sl0], in_=ngm4[:, sl0, 0:SC_CH], axis=AX.X,
                            negate=True),
                         reads=(f"ngm4{sl0}",), writes=(f"ngm{sl0}",))
                    if bko == 4 and SC_CH == 4:
                        P.op("scalar", lambda sce, sl0=sl0: sce.activation(
                                out=attn[:, sl0, 0:1024], in_=p_cx[:, 0:1024],
                                func=AF.Exp, bias=ngm[:, sl0],
                                accum_out=zz[:, sa0, 0:1]),
                             reads=("pb4", "pb5", f"ngm{sl0}"),
                             writes=(f"attn{sl0}", f"zz{sa0}"))
                        P.op("scalar", lambda sce, sl0=sl0: sce.activation(
                                out=attn[:, sl0, 1024:2048], in_=p_ff[:, 0:1024],
                                func=AF.Exp, bias=ngm[:, sl0],
                                accum_out=zz[:, sa0, 1:2]),
                             reads=("pb6", "pb7", f"ngm{sl0}"),
                             writes=(f"attn{sl0}", f"zzb{sa0}"))
                        P.op("vector", lambda v, sa0=sa0: v.tensor_tensor(
                                out=zz[:, sa0, 0:1], in0=zz[:, sa0, 0:1],
                                in1=zz[:, sa0, 1:2], op=ALU.add),
                             reads=(f"zz{sa0}", f"zzb{sa0}"), writes=(f"zz{sa0}",))
                    elif bko == 4:
                        P.op("scalar", lambda sce, sl0=sl0: sce.activation(
                                out=attn[:, sl0], in_=p_cx[:, 0:S], func=AF.Exp,
                                bias=ngm[:, sl0], accum_out=zz[:, sa0, 0:1]),
                             reads=tuple(f"pb{4 + c}" for c in range(SC_CH))
                                   + (f"ngm{sl0}",),
                             writes=(f"attn{sl0}", f"zz{sa0}"))
                    else:
                        P.op("scalar", lambda sce, sl0=sl0: sce.activation(
                                out=attn[:, sl0], in_=p_sc[:, 0:S], func=AF.Exp,
                                bias=ngm[:, sl0], accum_out=zz[:, sa0, 0:1]),
                             reads=tuple(f"pb{c}" for c in range(SC_CH)) + (f"ngm{sl0}",),
                             writes=(f"attn{sl0}", f"zz{sa0}"))
                    dma_tr(attnT[:, sa0], attn[:, sl0], f"strA{sa0}",
                           reads=(f"attn{sl0}",), writes=(f"attnT{sa0}",))
                    dma("scalar", xa[:, sx0], x_t[s, t0], f"sz{sx0}", writes=(f"xa{sx0}",))

                def emit_stage2(it=it):
                    nonlocal l2tr_pending
                    if not (2 <= it <= T + 1):
                        return
                    t1 = it - 2
                    sl1 = t1 % 2
                    sa1 = t1 % 3
                    sx1 = t1 % 4
                    sq1 = t1 % 4
                    pcx, pck = p_cx[:, 0:1024], ("pb4", "pb5")
                    def cx_half(hf):
                        def fn(te, sa1=sa1, pcx=pcx, hf=hf):
                            last = None
                            for c in range(T):
                                last = te.matmul(pcx[:, hf * 512:hf * 512 + 512],
                                                 attnT[:, sa1, c, :],
                                                 Vb[:, c, hf * 512:hf * 512 + 512],
                                                 start=(c == 0), stop=(c == T - 1))
                            return last
                        return fn
                    P.op("vector", lambda v, sl1=sl1, sa1=sa1: v.reciprocal(
                            out=rz[:, sl1], in_=zz[:, sa1, 0:1]),
                         reads=(f"zz{sa1}",), writes=(f"rz{sl1}",))
                    if t1 >= T - 4:
                        P.op("tensor", cx_half(0),
                             reads=(f"attnT{sa1}",) + tuple(f"Vb{c}" for c in range(T)),
                             writes=(pck[0],))
                        P.op("tensor", cx_half(1),
                             reads=(f"attnT{sa1}",) + tuple(f"Vb{c}" for c in range(T)),
                             writes=(pck[1],))
                        for hf in range(2):
                            P.op("vector", lambda v, sl1=sl1, sx1=sx1, sq1=sq1,
                                 pcx=pcx, hf=hf: v.scalar_tensor_tensor(
                                    out=x2[:, sq1, hf * 512:hf * 512 + 512],
                                    in0=pcx[:, hf * 512:hf * 512 + 512],
                                    scalar=rz[:, sl1],
                                    in1=xa[:, sx1, hf * 512:hf * 512 + 512],
                                    op0=ALU.mult, op1=ALU.add),
                                 reads=(pck[hf], f"rz{sl1}", f"xa{sx1}"),
                                 writes=(f"x2{sq1}h{hf}", f"x2{sq1}"))
                            P.op("vector", lambda v, sl1=sl1, sq1=sq1, hf=hf: v.bn_stats(
                                    out=st2[:, sl1, hf],
                                    in_=x2[:, sq1, hf * 512:hf * 512 + 512]),
                                 reads=(f"x2{sq1}h{hf}",), writes=(f"st2{sl1}",))
                        P.op("vector", lambda v, sl1=sl1: v.bn_aggr(
                                out=mv2[:, sl1], in_=st2[:, sl1]),
                             reads=(f"st2{sl1}",), writes=(f"mv2{sl1}",))
                        P.op("scalar", lambda sce, sl1=sl1: sce.activation(
                                out=rsd2[:, sl1, 0:1], in_=mv2[:, sl1, 1:2],
                                func=AF.Ln, bias=epsc[:]),
                             reads=(f"mv2{sl1}", "epsc"), writes=(f"rsd2{sl1}",))
                        P.op("scalar", lambda sce, sl1=sl1: sce.activation(
                                out=rsd2[:, sl1, 1:2], in_=rsd2[:, sl1, 0:1],
                                func=AF.Exp, scale=-0.5),
                             reads=(f"rsd2{sl1}",), writes=(f"rsd2{sl1}",))
                    else:
                        P.op("tensor", lambda te, sa1=sa1, pcx=pcx: (
                                cx_half(0)(te, sa1=sa1, pcx=pcx),
                                cx_half(1)(te, sa1=sa1, pcx=pcx))[-1],
                             reads=(f"attnT{sa1}",) + tuple(f"Vb{c}" for c in range(T)),
                             writes=pck)
                        P.op("vector", lambda v, sl1=sl1, sx1=sx1, sq1=sq1, pcx=pcx: v.scalar_tensor_tensor(
                                out=x2[:, sq1], in0=pcx[:, 0:1024], scalar=rz[:, sl1],
                                in1=xa[:, sx1], op0=ALU.mult, op1=ALU.add),
                             reads=pck + (f"rz{sl1}", f"xa{sx1}"),
                             writes=(f"x2{sq1}",))
                        ln_stats(sl1, x2[:, sq1], st2, mv2, rsd2,
                                 (f"x2{sq1}", f"st2{sl1}", f"mv2{sl1}", f"rsd2{sl1}"))
                    P.op("vector", lambda v, sl1=sl1, sq1=sq1: v.tensor_scalar(
                            out=ln2[:, sl1], in0=x2[:, sq1],
                            scalar1=mv2[:, sl1, 0:1], scalar2=rsd2[:, sl1, 1:2],
                            op0=ALU.subtract, op1=ALU.mult),
                         reads=(f"x2{sq1}", f"rsd2{sl1}", f"mv2{sl1}"),
                         writes=(f"ln2{sl1}",))
                    if t1 >= T - 4:
                        pbq = p_sc[:, 0:1024] if t1 % 2 == 0 else p_sc[:, 1024:2048]
                        pkq = ("pb0", "pb1") if t1 % 2 == 0 else ("pb2", "pb3")
                        def emit_l2tr(sl1=sl1, sq1=sq1, pbq=pbq, pkq=pkq,
                                      split=(t1 >= T - 2)):
                            def l2tr(te, sl1=sl1, pbq=pbq):
                                last = None
                                for hh in range(HC):
                                    last = te.matmul(pbq[:, hh * 128:(hh + 1) * 128],
                                                     ln2[:, sl1, hh * 128:(hh + 1) * 128],
                                                     ids[:], start=True, stop=True)
                                return last
                            P.op("tensor", l2tr, reads=(f"ln2{sl1}", "ids"), writes=pkq)
                            if split:
                                for hf in range(2):
                                    P.op("scalar", lambda sce, sq1=sq1, pbq=pbq,
                                         hf=hf: sce.activation(
                                            out=ln2q[:, sq1, 4 * hf:4 * hf + 4, :],
                                            in_=pbq[:, hf * 512:hf * 512 + 512],
                                            func=AF.Copy, scale=scs[:, 0:1]),
                                         reads=(pkq[hf], "scs"),
                                         writes=(f"ln2q{sq1}h{hf}", f"ln2q{sq1}"))
                            else:
                                P.op("scalar", lambda sce, sq1=sq1, pbq=pbq: sce.activation(
                                        out=ln2q[:, sq1], in_=pbq[:, 0:1024], func=AF.Copy,
                                        scale=scs[:, 0:1]),
                                     reads=pkq + ("scs",),
                                     writes=(f"ln2q{sq1}", f"ln2q{sq1}h0",
                                             f"ln2q{sq1}h1"))
                        l2tr_pending = emit_l2tr
                    else:
                        dma_tr(ln2T[:, sl1], ln2[:, sl1], f"strL{sl1}",
                               reads=(f"ln2{sl1}",), writes=(f"ln2T{sl1}",))
                        P.op("scalar", lambda sce, sl1=sl1, sq1=sq1: sce.activation(
                                out=ln2q[:, sq1], in_=ln2T[:, sl1], func=AF.Copy,
                                scale=scs[:, 0:1]),
                             reads=(f"ln2T{sl1}", "scs"),
                             writes=(f"ln2q{sq1}", f"ln2q{sq1}h0", f"ln2q{sq1}h1"))

                emit_stage1()
                emit_stage2()

                if l2tr_flush is not None and it <= T + 1:
                    l2tr_flush()
                    l2tr_flush = None

                if s + 1 < B:
                    kl = it
                    if kl < T:
                        dma("scalar", xb[:, kl % 2], x_t[s + 1, kl], f"sxb{kl % 2}",
                            writes=(f"xb{kl % 2}",))
                    ks = it - 1
                    if 0 <= ks < T:
                        kslot = ks % 2
                        P.op("vector", lambda v, ks=ks, kslot=kslot: v.bn_stats(
                                out=st1b[:, kslot, 0], in_=xb[:, kslot, 0:512]),
                             reads=(f"xb{kslot}",), writes=(f"st1b{kslot}",))
                        P.op("vector", lambda v, ks=ks, kslot=kslot: v.bn_stats(
                                out=st1b[:, kslot, 1], in_=xb[:, kslot, 512:1024]),
                             reads=(f"xb{kslot}",), writes=(f"st1b{kslot}",))
                        P.op("vector", lambda v, ks=ks, kslot=kslot: v.bn_aggr(
                                out=mv1b[:, ks], in_=st1b[:, kslot]),
                             reads=(f"st1b{kslot}",), writes=(f"mv1b{ks}",))
                        def emit_pref_rsd(ks=ks):
                            P.op("scalar", lambda sce, ks=ks: sce.activation(
                                    out=rsd1b[:, ks, 0:1], in_=mv1b[:, ks, 1:2],
                                    func=AF.Ln, bias=epsc[:]),
                                 reads=(f"mv1b{ks}", "epsc"), writes=(f"rsd1b{ks}",))
                            P.op("scalar", lambda sce, ks=ks: sce.activation(
                                    out=rsd1b[:, ks, 1:2], in_=rsd1b[:, ks, 0:1],
                                    func=AF.Exp, scale=-0.5),
                                 reads=(f"rsd1b{ks}",), writes=(f"rsd1b{ks}",))
                        pref_rsd_pending = emit_pref_rsd

                if 4 <= it <= T + 3:
                    t2 = it - 4
                    sl2 = t2 % 2
                    sq2 = t2 % 4
                    DR = mybir.MatmulPerfMode.DoubleRow
                    def ff_mm(te, sq2=sq2):
                        last = None
                        for j in range(HC // 2):
                            last = te.matmul(p_ff[:, 0:512], ln2q[:, sq2, 2 * j:2 * j + 2, :],
                                             kq[:, 2 * j:2 * j + 2, 0:512],
                                             start=(j == 0), stop=(j == HC // 2 - 1),
                                             perf_mode=DR)
                        for j in range(HC // 2):
                            last = te.matmul(p_ff[:, 512:1024], ln2q[:, sq2, 2 * j:2 * j + 2, :],
                                             kq[:, 2 * j:2 * j + 2, 512:1024],
                                             start=(j == 0), stop=(j == HC // 2 - 1),
                                             perf_mode=DR)
                        return last
                    if t2 >= T - 2:
                        for hf in range(2):
                            def ff_h(te, sq2=sq2, hf=hf):
                                last = None
                                for j in (2 * hf, 2 * hf + 1):
                                    last = te.matmul(p_ff[:, 0:512],
                                                     ln2q[:, sq2, 2 * j:2 * j + 2, :],
                                                     kq[:, 2 * j:2 * j + 2, 0:512],
                                                     start=(j == 0),
                                                     stop=(j == HC // 2 - 1),
                                                     perf_mode=DR)
                                for j in (2 * hf, 2 * hf + 1):
                                    last = te.matmul(p_ff[:, 512:1024],
                                                     ln2q[:, sq2, 2 * j:2 * j + 2, :],
                                                     kq[:, 2 * j:2 * j + 2, 512:1024],
                                                     start=(j == 0),
                                                     stop=(j == HC // 2 - 1),
                                                     perf_mode=DR)
                                return last
                            P.op("tensor", ff_h, reads=(f"ln2q{sq2}h{hf}", "kq"),
                                 writes=("pb6", "pb7"))
                    else:
                        P.op("tensor", ff_mm, reads=(f"ln2q{sq2}", "kq"),
                             writes=("pb6", "pb7"))
                    P.op("scalar", lambda sce, sl2=sl2: sce.activation(
                            out=y1[:, sl2], in_=p_ff[:, 0:1024], func=AF.Copy,
                            scale=scs[:, 2:3]),
                         reads=("pb6", "pb7", "scs"), writes=(f"y1{sl2}",))
                    if s == B - 1 and t2 >= T - 2:
                        P.op("vector", lambda v, sl2=sl2: v.tensor_tensor(
                                out=yy[:, sl2], in0=y1[:, sl2], in1=fbb[:],
                                op=ALU.add),
                             reads=(f"y1{sl2}", "fbb"), writes=(f"yy{sl2}",))
                        for h in range(2):
                            P.op("vector", lambda v, sl2=sl2, sq2=sq2, h=h:
                                 v.tensor_tensor(
                                    out=xo[:, sl2, h * 512:(h + 1) * 512],
                                    in0=x2[:, sq2, h * 512:(h + 1) * 512],
                                    in1=yy[:, sl2, h * 512:(h + 1) * 512],
                                    op=ALU.add),
                                 reads=(f"x2{sq2}", f"yy{sl2}"),
                                 writes=(f"xo{sl2}h{h}", f"xo{sl2}"))
                        dma("gpsimd", y_t[s, t2][:, 0:512], xo[:, sl2, 0:512],
                            f"sst{sl2}", reads=(f"xo{sl2}h0",))
                        dma("sync", y_t[s, t2][:, 512:1024], xo[:, sl2, 512:1024],
                            "ssy", reads=(f"xo{sl2}h1",))
                    else:
                        P.op("gpsimd", lambda g, sl2=sl2: g.tensor_tensor(
                                out=yy[:, sl2], in0=y1[:, sl2], in1=fbb[:], op=ALU.add),
                             reads=(f"y1{sl2}", "fbb"), writes=(f"yy{sl2}",))
                        P.op("gpsimd", lambda g, sl2=sl2, sq2=sq2: g.tensor_tensor(
                                out=xo[:, sl2], in0=x2[:, sq2], in1=yy[:, sl2], op=ALU.add),
                             reads=(f"x2{sq2}", f"yy{sl2}"), writes=(f"xo{sl2}",))
                        dma("gpsimd", y_t[s, t2], xo[:, sl2], f"sst{sl2}",
                            reads=(f"xo{sl2}",))

                if pref_flush is not None:
                    pref_flush()

                if l2tr_flush is not None:
                    l2tr_flush()

        P.wait("gpsimd", "sst0", P.semcnt["sst0"])
        P.wait("gpsimd", "sst1", P.semcnt["sst1"])
        P.wait("gpsimd", "ssy", P.semcnt["ssy"])

        with nc.Block() as block:
            def mk(eng_name):
                def run(eng):
                    for item in P.streams[eng_name]:
                        if item[0] == "w":
                            eng.wait_ge(semh[item[1]], item[2])
                        else:
                            _, fn, semn, inc, self_inc = item
                            instr = fn(eng)
                            if self_inc:
                                instr.then_inc(semh[semn], inc)
                return run
            block.sync(mk("sync"))
            block.vector(mk("vector"))
            block.scalar(mk("scalar"))
            block.tensor(mk("tensor"))
            block.gpsimd(mk("gpsimd"))

    return nc


def kernel(L0_attn_lstm_out, L0_attn_W, L0_attn_b, L0_ffn_kernel,
           L0_ffn_input_scale, L0_ffn_kernel_scale, L0_ffn_bias):
    bf16 = ml_dtypes.bfloat16
    x = np.ascontiguousarray(L0_attn_lstm_out, dtype=np.float32)
    WT = np.ascontiguousarray(np.asarray(L0_attn_W, dtype=np.float32).T.astype(bf16))
    b = np.ascontiguousarray(L0_attn_b, dtype=np.float32)
    ker = np.ascontiguousarray(np.asarray(L0_ffn_kernel).astype(bf16))
    sc = np.array([np.float32(L0_ffn_input_scale[0]),
                   np.float32(L0_ffn_kernel_scale[0])], dtype=np.float32)
    fb = np.ascontiguousarray(np.asarray(L0_ffn_bias).astype(bf16))
    ids = np.eye(128, dtype=bf16)

    BATCH, S, _ = x.shape
    per = BATCH // N_CORES
    nc = build_nc(per, S)
    in_maps = []
    for i in range(N_CORES):
        in_maps.append({
            "x": np.ascontiguousarray(x[i * per:(i + 1) * per]),
            "wt": WT, "b": b, "ker": ker, "sc": sc, "fb": fb, "ids": ids,
        })
    res = run_bass_kernel_spmd(nc, in_maps, core_ids=list(range(N_CORES)))
    out = np.concatenate([np.asarray(r["y"]) for r in res.results], axis=0)
    return out.astype(np.float32)
